# version 34
# baseline (speedup 1.0000x reference)
import numpy as np
import ml_dtypes
import concourse.bass as bass
import concourse.mybir as mybir
from concourse.bass_utils import run_bass_kernel_spmd
from concourse.alu_op_type import AluOpType

F32 = mybir.dt.float32
BF16 = mybir.dt.bfloat16
AFT = mybir.ActivationFunctionType
AX = mybir.AxisListType

CF_IN_MAIN = True
PREP_IN_MAIN = True
NB = 2
SEQ = 2048
D = 1024
DC = 8
NT = 4
NQ = 16

class _Op:
    __slots__ = ("stream", "chan", "fn", "dma", "pos", "chan_idx", "marked",
                 "waits", "drain", "clock", "tag")

class Sched:
    def __init__(self, streams, total_chans=()):
        self.streams = {s: [] for s in streams}
        self.total_chans = set(total_chans)
        self.chan_count = {}
        self.res = {}
        self.stream_clock = {s: {} for s in streams}
        self.last_drain = {s: -1 for s in streams}

    def add(self, stream, fn, reads=(), writes=(), chan=None, dma=False, tag=""):
        chan = chan or stream
        op = _Op()
        op.stream, op.chan, op.fn, op.dma, op.tag = stream, chan, fn, dma, tag
        op.pos = len(self.streams[stream])
        op.chan_idx = self.chan_count.get(chan, 0)
        self.chan_count[chan] = op.chan_idx + 1
        op.marked = dma
        op.waits = {}
        op.drain = False
        deps = []
        for k in reads:
            st = self.res.get(k)
            if st and st[0] is not None:
                deps.append((st[0], False))
        for k in writes:
            st = self.res.get(k)
            if st:
                if st[0] is not None:
                    deps.append((st[0], False))
                for r in st[1].values():
                    deps.append((r, True))
        clock = dict(self.stream_clock[stream])
        for dep, is_war in deps:
            if dep is op:
                continue
            if dep.dma:
                if dep.chan in self.total_chans:
                    idx = self.chan_count[dep.chan] - 1
                else:
                    idx = dep.chan_idx
                if clock.get(dep.chan, -1) < idx:
                    op.waits[dep.chan] = max(op.waits.get(dep.chan, -1), idx)
                    for c2, i2 in dep.clock.items():
                        if clock.get(c2, -1) < i2:
                            clock[c2] = i2
                    clock[dep.chan] = idx
            elif dep.stream == stream:
                if not is_war and stream != "T":
                    if clock.get(dep.chan, -1) < dep.chan_idx:
                        dep.marked = True
                        op.waits[dep.chan] = max(op.waits.get(dep.chan, -1), dep.chan_idx)
                        clock[dep.chan] = dep.chan_idx
            else:
                if clock.get(dep.chan, -1) < dep.chan_idx:
                    dep.marked = True
                    op.waits[dep.chan] = max(op.waits.get(dep.chan, -1), dep.chan_idx)
                    for c2, i2 in dep.clock.items():
                        if clock.get(c2, -1) < i2:
                            clock[c2] = i2
                    clock[dep.chan] = dep.chan_idx
        op.clock = clock
        self.stream_clock[stream] = clock
        for k in reads:
            self.res.setdefault(k, [None, {}])[1][stream] = op
        for k in writes:
            self.res[k] = [op, {}]
        self.streams[stream].append(op)
        return op

    def finalize(self):
        self.mark_val = {}
        per_chan_ops = {}
        for s, ops in self.streams.items():
            for o in ops:
                per_chan_ops.setdefault(o.chan, []).append(o)
        for c, ops in per_chan_ops.items():
            ops.sort(key=lambda o: o.chan_idx)
            acc = 0
            vals = []
            for o in ops:
                if o.marked:
                    acc += 16 if o.dma else 1
                vals.append(acc)
            self.mark_val[c] = vals

    def wait_value(self, chan, idx):
        return self.mark_val[chan][idx]

    def emit(self, stream, eng, sems):
        for o in self.streams[stream]:
            if o.drain:
                eng.drain()
            for c, idx in sorted(o.waits.items(), key=lambda kv: str(kv[0])):
                v = self.wait_value(c, idx)
                if v > 0:
                    eng.wait_ge(sems[c], v)
            inst = o.fn(eng)
            if o.marked:
                if inst is None:
                    inst = eng.nop()
                inst.then_inc(sems[o.chan], 16 if o.dma else 1)


def build_program():
    nc = bass.Bass("TRN2", target_bir_lowering=False)
    from contextlib import ExitStack
    es = ExitStack()

    x = nc.dram_tensor("x", [NB, SEQ, D], F32, kind="ExternalInput")
    W1 = nc.dram_tensor("W1", [128, 8], F32, kind="ExternalInput")
    b1 = nc.dram_tensor("b1", [128, 8], F32, kind="ExternalInput")
    W2 = nc.dram_tensor("W2", [2 * D, D], F32, kind="ExternalInput")
    U1 = nc.dram_tensor("U1", [D, 256], F32, kind="ExternalInput")
    V1 = nc.dram_tensor("V1", [256, 4096], F32, kind="ExternalInput")
    fb1 = nc.dram_tensor("fb1", [128, 32], F32, kind="ExternalInput")
    U2 = nc.dram_tensor("U2", [4096, 256], F32, kind="ExternalInput")
    V2 = nc.dram_tensor("V2", [256, D], F32, kind="ExternalInput")
    b2 = nc.dram_tensor("b2", [1, D], F32, kind="ExternalInput")
    idb = nc.dram_tensor("idb", [128, 128], BF16, kind="ExternalInput")
    idf = nc.dram_tensor("idf", [128, 128], F32, kind="ExternalInput")
    out = nc.dram_tensor("out", [NB, SEQ, D], F32, kind="ExternalOutput")

    def sb(name, shape, dt=F32):
        return es.enter_context(nc.sbuf_tensor(name, shape, dt))

    def ps(name, shape, dt=F32):
        return es.enter_context(nc.psum_tensor(name, shape, dt))

    w2sb = sb("w2sb", [128, 16 * 1024], BF16)
    u1sb = sb("u1sb", [128, 8 * 256], BF16)
    v1sb = sb("v1sb", [128, 2 * 4096], BF16)
    u2sb = sb("u2sb", [128, 32 * 256], BF16)
    v2sb = sb("v2sb", [128, 2 * 1024], BF16)
    wstage = [sb(f"wstage{i}", [128, 2048]) for i in range(2)]
    w1c = sb("w1c", [128, 8])
    b1c = sb("b1c", [128, 8])
    fb1c = sb("fb1c", [128, 32])
    cscr = sb("cscr", [1, 512])
    b2b = sb("b2b", [128, 1024])
    onesf = sb("onesf", [1, 128])
    onesel = sb("onesel", [128, 16], BF16)
    identbf = sb("identbf", [128, 128], BF16)
    identf = sb("identf", [128, 128])

    nfm = sb("nfm", [128, 8 * 2048], BF16)
    xa = [sb(f"xa{i}", [128, 1024]) for i in range(2)]
    xr = [sb(f"xr{i}", [128, 1024]) for i in range(2)]
    xb = [wstage[0][:, 0:1024], wstage[0][:, 1024:2048]]
    outsb = [sb(f"outsb{i}", [128, 1024]) for i in range(3)]
    xnfm = [sb(f"xnfm{i}", [128, 8 * 512], BF16) for i in range(2)]
    ntok = [sb(f"ntok{i}", [128, 1024], BF16) for i in range(2)]
    xntok = [sb(f"xntok{i}", [128, 1024], BF16) for i in range(4)]
    pfm = [sb(f"pfm{i}", [128, 2 * 512], BF16) for i in range(2)]
    qfm = sb("qfm", [128, 2 * 512], BF16)
    hbf = [sb(f"hbf{i}", [128, 512], BF16) for i in range(2)]
    e512 = [wstage[1][:, 0:512], wstage[1][:, 512:1024]]
    vjunk = sb("vjunk", [128, 2048], BF16)
    cpart = [sb(f"cpart{i}", [128, 4]) for i in range(2)]
    call = sb("call", [128, 16])
    crow4 = [sb(f"crow4{i}", [4, 128]) for i in range(2)]
    crow1 = sb("crow1", [1, 2048])
    cols = {}
    for n in ["sum", "ssq", "mean", "var", "u", "ru", "sq", "rs", "rs2",
              "scaleA", "biasA", "invs", "biasB", "t1", "t2"]:
        cols[n] = [sb(f"c_{n}{p}", [128, 16]) for p in range(2)]
    sq32 = [sb(f"sq32_{p}", [128, 32]) for p in range(2)]
    iv32 = [sb(f"iv32_{p}", [128, 32]) for p in range(2)]

    s0 = ps("s0", [128, 512]); s1 = ps("s1", [128, 512])
    s2 = ps("s2", [128, 512]); s3 = ps("s3", [128, 512])
    q0 = ps("q0", [128, 512]); q1 = ps("q1", [128, 512])
    tb = [ps("tb0", [128, 1024], BF16), ps("tb1", [128, 1024], BF16)]
    sbank = [s0, s1, s2, s3]

    streams = ["T", "V", "S", "G", "SY"]
    dma_chans = (["QC"] + [("QW", i) for i in range(2)] + [("QXA", i) for i in range(2)]
                 + [("QXR", i) for i in range(2)] + [("QXO", i) for i in range(2)]
                 + [("QTN", i) for i in range(2)]
                 + [("QST", i) for i in range(3)])
    sch = Sched(streams, total_chans=["QC"])
    sems = {}
    for c in streams + dma_chans:
        nm = c if isinstance(c, str) else c[0] + str(c[1])
        sems[c] = es.enter_context(nc.semaphore("sem_" + nm))

    A = sch.add

    K_nfm = [("nfm", t) for t in range(NT)]
    K_crow1 = [("crow1", t) for t in range(NT)]
    K_w = "wALL"
    A("SY", lambda e: e.dma_start(out=identbf[:, :], in_=idb[:, :]),
      writes=[("identbf",)], chan="QC", dma=True)
    A("SY", lambda e: e.dma_start(out=identf[:, :], in_=idf[:, :]),
      writes=[("identf",)], chan="QC", dma=True)
    A("SY", lambda e: e.dma_start(out=w1c[:, :], in_=W1[:, :]),
      writes=[("w1c",)], chan="QC", dma=True)
    A("SY", lambda e: e.dma_start(out=b1c[:, :], in_=b1[:, :]),
      writes=[("b1c",)], chan="QC", dma=True)
    A("SY", lambda e: e.dma_start(out=fb1c[:, :], in_=fb1[:, :]),
      writes=[("fb1c",)], chan="QC", dma=True)
    b2bc, _ = bass.broadcast_tensor_aps(b2[0:1, :], b2b[:, :])
    A("SY", lambda e: e.dma_start(out=b2b[:, :], in_=b2bc),
      writes=[("b2b",)], chan="QC", dma=True)
    A("V", lambda e: e.memset(onesf[:, :], 1.0), writes=[("onesf",)])
    A("V", lambda e: e.memset(onesel[:, :], 0.0), writes=[("onesel",)])
    for kc in range(4):
        A("V", lambda e, kc=kc: e.memset(onesel[:, 4 * kc + kc:4 * kc + kc + 1], 1.0),
          reads=[("onesel",)], writes=[("onesel",)])

    def xslot(st, nslot):
        i = st % nslot
        if i < 2:
            return xa[i], ("xa", i), ("QXA", i)
        if i < 4:
            return xr[i - 2], ("xr", i - 2), ("QXR", i - 2)
        return outsb[i - 4], ("outsb", i - 4), ("QXO", i - 4)

    def emit_stats(b, tiles, nslot=4, fused=False, sum_on_v=False):
        p = b % 2
        for t in tiles:
            for s_ in range(4):
                st = 4 * t + s_
                buf, bkey, bchan = xslot(st, nslot)
                A("SY", lambda e, b=b, st=st, buf=buf:
                  e.dma_start(out=buf[:, :], in_=x[b, st * 128:(st + 1) * 128, :]),
                  reads=[], writes=[bkey], chan=bchan, dma=True)
                if sum_on_v:
                    A("V", lambda e, buf=buf, p=p, st=st:
                      e.tensor_scalar(out=vjunk[:, 0:1024], in0=buf[:, :], scalar1=0.0,
                                      scalar2=None, op0=AluOpType.add, op1=AluOpType.add,
                                      accum_out=cols["sum"][p][:, st:st + 1]),
                      reads=[bkey], writes=[("c_sum", p, st), ("vj", 0), ("vj", 1)])
                else:
                    A("S", lambda e, buf=buf, p=p, st=st:
                      e.activation(out=outsb[2][:, :], in_=buf[:, :], func=AFT.Identity,
                                   accum_out=cols["sum"][p][:, st:st + 1]),
                      reads=[bkey], writes=[("c_sum", p, st), ("outsb", 2)])
                A("S", lambda e, buf=buf, p=p, st=st:
                  e.activation(out=outsb[2][:, :], in_=buf[:, :], func=AFT.Square,
                               accum_out=cols["ssq"][p][:, st:st + 1]),
                  reads=[bkey], writes=[("c_ssq", p, st), ("outsb", 2)])
            if fused:
                emit_colmath(b, 4 * t, 4 * t + 4)
                C = {n: cols[n][p] for n in cols}
                for s_ in range(4):
                    st = 4 * t + s_
                    buf, bkey, bchan = xslot(st, nslot)
                    nt_ = ntok[st % 2]
                    A("V", lambda e, buf=buf, nt_=nt_, st=st, C=C:
                      e.tensor_scalar(out=nt_[:, :], in0=buf[:, :],
                                      scalar1=C["scaleA"][:, st:st + 1],
                                      scalar2=C["biasA"][:, st:st + 1],
                                      op0=AluOpType.mult, op1=AluOpType.add),
                      reads=[bkey, ("cmA", p, st)], writes=[("ntok", st % 2)])
                    bank = tb[st % 2]
                    for dc in range(DC):
                        A("T", lambda e, nt_=nt_, bank=bank, dc=dc:
                          e.transpose(bank[:, dc * 128:(dc + 1) * 128],
                                      nt_[:, dc * 128:(dc + 1) * 128], identbf[:, :]),
                          reads=[("ntok", st % 2), ("identbf",)], writes=[("ps_tb", st % 2)])
                    A("V", lambda e, bank=bank, t=t, s_=s_:
                      e.tensor_copy(nfm.rearrange("p (c f) -> p c f", c=8)[:, :, t * 512 + s_ * 128: t * 512 + (s_ + 1) * 128],
                                    bank[:, :].rearrange("p (c f) -> p c f", c=8)),
                      reads=[("ps_tb", st % 2)], writes=[("nfm", t)])

    def emit_colmath(b, s0, s1):
        p = b % 2
        C = {n: cols[n][p][:, s0:s1] for n in cols}
        K = lambda n: ("cm", p, n, s0)
        rd_sum = [("c_sum", p, st) for st in range(s0, s1)]
        rd_ssq = [("c_ssq", p, st) for st in range(s0, s1)]
        def vop(fn, reads, writes):
            A("V", fn, reads=reads, writes=writes)
        vop(lambda e: e.tensor_scalar(out=C["mean"], in0=C["sum"],
            scalar1=1.0 / D, scalar2=None, op0=AluOpType.mult), rd_sum, [K("mean")])
        vop(lambda e: e.tensor_scalar(out=C["t1"], in0=C["ssq"],
            scalar1=1.0 / D, scalar2=None, op0=AluOpType.mult), rd_ssq, [K("t1")])
        vop(lambda e: e.tensor_tensor(out=C["t2"], in0=C["mean"],
            in1=C["mean"], op=AluOpType.mult), [K("mean")], [K("t2")])
        vop(lambda e: e.tensor_tensor(out=C["var"], in0=C["t1"],
            in1=C["t2"], op=AluOpType.subtract), [K("t1"), K("t2")], [K("var")])
        vop(lambda e: e.tensor_scalar(out=C["u"], in0=C["var"],
            scalar1=1e-5, scalar2=None, op0=AluOpType.add), [K("var")], [K("u")])
        vop(lambda e: e.reciprocal(C["ru"], C["u"]), [K("u")], [K("ru")])
        vop(lambda e: e.tensor_tensor(out=C["t1"], in0=C["var"],
            in1=C["ru"], op=AluOpType.mult), [K("var"), K("ru")], [K("t1")])
        vop(lambda e: e.tensor_scalar(out=C["sq"], in0=C["t1"],
            scalar1=float(D), scalar2=1e-12, op0=AluOpType.mult, op1=AluOpType.max),
            [K("t1")], [K("sq")])
        vop(lambda e: e.reciprocal(C["rs"], C["sq"]), [K("sq")], [K("rs")])
        sqa = sq32[p][:, s0:s1]; sqb = sq32[p][:, 16 + s0:16 + s1]
        iva = iv32[p][:, s0:s1]; ivb = iv32[p][:, 16 + s0:16 + s1]
        vop(lambda e: e.tensor_copy(sqa, C["ru"]), [K("ru")], [K("s32a")])
        vop(lambda e: e.tensor_copy(sqb, C["rs"]), [K("rs")], [K("s32b")])
        A("S", lambda e: e.activation(out=iva, in_=sqa, func=AFT.Sqrt),
          reads=[K("s32a")], writes=[K("iva")])
        A("S", lambda e: e.activation(out=ivb, in_=sqb, func=AFT.Sqrt),
          reads=[K("s32b")], writes=[K("ivb")])
        def newton(ycols, ykey, acol, akey, t1, outc, outk):
            vop(lambda e: e.tensor_tensor(out=t1, in0=ycols, in1=ycols,
                op=AluOpType.mult), [ykey], [K(outk + "t1")])
            vop(lambda e: e.tensor_tensor(out=t1, in0=t1, in1=acol,
                op=AluOpType.mult), [K(outk + "t1"), akey], [K(outk + "t1")])
            vop(lambda e: e.tensor_scalar(out=t1, in0=t1, scalar1=-0.5,
                scalar2=1.5, op0=AluOpType.mult, op1=AluOpType.add),
                [K(outk + "t1")], [K(outk + "t1")])
            vop(lambda e: e.tensor_tensor(out=outc, in0=ycols, in1=t1,
                op=AluOpType.mult), [ykey, K(outk + "t1")], [K(outk)])
        newton(iva, K("iva"), C["u"], K("u"), C["t1"], C["invs"], "invs")
        newton(ivb, K("ivb"), C["sq"], K("sq"), C["t2"], C["rs2"], "rs2")
        vop(lambda e: e.tensor_tensor(out=C["scaleA"], in0=C["invs"],
            in1=C["rs2"], op=AluOpType.mult),
            [K("invs"), K("rs2")], [K("scaleA")])
        vop(lambda e: e.scalar_tensor_tensor(out=C["biasA"], in0=C["mean"],
            scalar=-1.0, in1=C["scaleA"], op0=AluOpType.mult, op1=AluOpType.mult),
            [K("mean"), K("scaleA")], [K("biasA")] + [("cmA", p, st) for st in range(s0, s1)])
        vop(lambda e: e.scalar_tensor_tensor(out=C["biasB"], in0=C["mean"],
            scalar=-1.0, in1=C["invs"], op0=AluOpType.mult, op1=AluOpType.mult),
            [K("mean"), K("invs")], [K("biasB")] + [("cmB", p, st) for st in range(s0, s1)])

    def emit_phase1_tile(b, t, nslot=2, cast_eng="V"):
        p = b % 2
        C = {n: cols[n][p] for n in cols}
        for s_ in range(4):
            st = 4 * t + s_
            buf, bkey, bchan = xslot(st, nslot)
            A("SY", lambda e, b=b, st=st, buf=buf:
              e.dma_start(out=buf[:, :], in_=x[b, st * 128:(st + 1) * 128, :]),
              writes=[bkey], chan=bchan, dma=True)
            nt_ = ntok[st % 2]
            if cast_eng == "V":
                A("V", lambda e, buf=buf, nt_=nt_, st=st:
                  e.tensor_scalar(out=nt_[:, :], in0=buf[:, :],
                                  scalar1=C["scaleA"][:, st:st + 1], scalar2=C["biasA"][:, st:st + 1],
                                  op0=AluOpType.mult, op1=AluOpType.add),
                  reads=[bkey, ("cmA", p, st)],
                  writes=[("ntok", st % 2)])
            else:
                A("S", lambda e, buf=buf, nt_=nt_, st=st:
                  e.activation(out=nt_[:, :], in_=buf[:, :], func=AFT.Identity,
                               scale=C["scaleA"][:, st:st + 1], bias=C["biasA"][:, st:st + 1]),
                  reads=[bkey, ("cmA", p, st)],
                  writes=[("ntok", st % 2)])
            bank = tb[st % 2]
            for dc in range(DC):
                A("T", lambda e, nt_=nt_, bank=bank, dc=dc:
                  e.transpose(bank[:, dc * 128:(dc + 1) * 128], nt_[:, dc * 128:(dc + 1) * 128],
                              identbf[:, :]),
                  reads=[("ntok", st % 2), ("identbf",)], writes=[("ps_tb", st % 2)])
            if cast_eng == "S":
                A("V", lambda e, bank=bank, t=t, s_=s_:
                  e.tensor_copy(nfm.rearrange("p (c f) -> p c f", c=8)[:, :, t * 512 + s_ * 128: t * 512 + (s_ + 1) * 128],
                                bank[:, :].rearrange("p (c f) -> p c f", c=8)),
                  reads=[("ps_tb", st % 2)], writes=[("nfm", t)], tag=f"nfmev{b}_{t}{s_}")
            else:
                A("S", lambda e, bank=bank, t=t, s_=s_:
                  e.copy(nfm.rearrange("p (c f) -> p c f", c=8)[:, :, t * 512 + s_ * 128: t * 512 + (s_ + 1) * 128],
                         bank[:, :].rearrange("p (c f) -> p c f", c=8)),
                  reads=[("ps_tb", st % 2)], writes=[("nfm", t)], tag=f"nfmev{b}_{t}{s_}")

    sim_state = {}

    def emit_sim_tile(b, t):
        if t == 0:
            sim_state["bc"] = 0
            sim_state["cs"] = 0
            sim_state["pending"] = []
            sim_state["ncs"] = sum(3 - i // 4 for i in range(16))
        def flush_pending():
            for (vslot, kc) in sim_state["pending"]:
                ci = sim_state["cs"]; sim_state["cs"] += 1
                A("T", lambda e, vslot=vslot, kc=kc, ci=ci:
                  e.matmul(q1[0:4, 0:512], onesel[:, 4 * kc:4 * kc + 4],
                           vjunk[:, vslot * 512:(vslot + 1) * 512],
                           start=(ci == 0), stop=(ci == sim_state["ncs"] - 1)),
                  reads=[("vj", vslot), ("onesel",)], writes=[("ps_q1",)], tag=f"cs{b}")
            sim_state["pending"] = []
        for qq in range(4):
            q = 4 * t + qq
            cp = cpart[q % 2]
            nb = 4 - q // 4
            for j, kc in enumerate(range(q // 4, 4)):
                bank = sbank[sim_state["bc"] % 4]
                bkkey = ("ps_s", sim_state["bc"] % 4)
                vslot = sim_state["bc"] % 4
                for dc in range(DC):
                    A("T", lambda e, bank=bank, dc=dc, q=q, kc=kc:
                      e.matmul(bank[:, :],
                               nfm[:, dc * 2048 + q * 128: dc * 2048 + (q + 1) * 128],
                               nfm[:, dc * 2048 + kc * 512: dc * 2048 + (kc + 1) * 512],
                               start=(dc == 0), stop=(dc == DC - 1)),
                      reads=[("nfm", tt) for tt in range(NT)],
                      writes=[bkkey], tag=f"sim{b}_{q}_{kc}")
                flush_pending()
                A("V", lambda e, bank=bank, cp=cp, j=j, vslot=vslot:
                  e.tensor_scalar(out=vjunk[:, vslot * 512:(vslot + 1) * 512],
                                  in0=bank[:, :], scalar1=0.0,
                                  scalar2=None, op0=AluOpType.max, op1=AluOpType.add,
                                  accum_out=cp[:, j:j + 1]),
                  reads=[bkkey], writes=[("cpart", q % 2, j), ("vj", vslot)])
                if kc > q // 4:
                    sim_state["pending"].append((vslot, kc))
                sim_state["bc"] += 1
            A("V", lambda e, cp=cp, q=q, nb=nb:
              e.reduce_sum(call[:, q:q + 1], cp[:, 0:nb], axis=AX.X),
              reads=[("cpart", q % 2, j) for j in range(nb)], writes=[("call", t)])
        c4 = crow4[t % 2]
        A("T", lambda e, t=t: e.transpose(q0[0:4, 0:128], call[:, 4 * t:4 * t + 4], identf[:, :]),
          reads=[("call", t), ("identf",)], writes=[("ps_q0",)])
        A("V", lambda e, c4=c4: e.tensor_copy(c4[:, :], q0[0:4, 0:128]),
          reads=[("ps_q0",)], writes=[("crow4", t % 2)])
        for qi in range(4):
            A("SY", lambda e, c4=c4, t=t, qi=qi:
              e.dma_start(out=crow1[0:1, t * 512 + qi * 128: t * 512 + (qi + 1) * 128],
                          in_=c4[qi:qi + 1, :]),
              reads=[("crow4", t % 2)], writes=[("crow1", t)], chan=("QTN", t % 2), dma=True)
        if t == NT - 1:
            flush_pending()
            A("V", lambda e: e.tensor_copy(wstage[1][0:4, 1024:1536], q1[0:4, 0:512]),
              reads=[("ps_q1",)], writes=[("colsb",), ("wstage", 1)])
            for kc in range(1, 4):
                A("SY", lambda e, kc=kc:
                  e.dma_start(out=cscr[0:1, 0:512], in_=wstage[1][kc:kc + 1, 1024:1536]),
                  reads=[("colsb",)], writes=[("cscr",)], chan=("QTN", 0), dma=True)
                A("V", lambda e, kc=kc:
                  e.tensor_tensor(out=crow1[0:1, kc * 512:(kc + 1) * 512],
                                  in0=crow1[0:1, kc * 512:(kc + 1) * 512],
                                  in1=cscr[0:1, 0:512], op=AluOpType.add),
                  reads=[("cscr",), ("crow1", kc)], writes=[("crow1", kc)])

    def emit_cf_tile(b, t):
        qb = [q0, q1][t % 2]
        qk = ("ps_q0",) if t % 2 == 0 else ("ps_q1",)
        A("T", lambda e, t=t, qb=qb: e.matmul(qb[:, :], onesf[:, :],
                                       crow1[0:1, t * 512:(t + 1) * 512], start=True, stop=True),
          reads=[("onesf",), ("crow1", t)], writes=[qk], tag=f"T2_{b}_{t}")
        for dc in range(DC):
            ebuf = e512[dc % 2]
            A("S", lambda e, ebuf=ebuf, dc=dc, qb=qb:
              e.activation(out=ebuf[:, :], in_=qb[:, :], func=AFT.Exp,
                           scale=w1c[:, dc:dc + 1], bias=b1c[:, dc:dc + 1]),
              reads=[qk, ("w1c",), ("b1c",)], writes=[("e512", dc % 2), ("wstage", 1)])
            A("S", lambda e, ebuf=ebuf, dc=dc, t=t:
              e.activation(out=nfm[:, dc * 2048 + t * 512: dc * 2048 + (t + 1) * 512],
                           in_=ebuf[:, :], func=AFT.Ln, bias=1.0, scale=1.0),
              reads=[("e512", dc % 2)], writes=[("nfm", t)], tag=f"cf{b}_{t}_{dc}")

    def emit_prep_xload(b, t):
        for s_ in range(4):
            st = 4 * t + s_
            bufr = xr[st % 2]
            A("SY", lambda e, b=b, st=st, bufr=bufr:
              e.dma_start(out=bufr[:, :], in_=x[b, st * 128:(st + 1) * 128, :]),
              writes=[("xr", st % 2)], chan=("QXR", st % 2), dma=True)
            emit_prep_cast(b, st)

    def emit_prep_cast(b, st):
        p = b % 2
        C = {n: cols[n][p] for n in cols}
        bufr = xr[st % 2]
        xt_ = xntok[st % 4]
        A("V", lambda e, bufr=bufr, xt_=xt_, st=st:
          e.tensor_scalar(out=xt_[:, :], in0=bufr[:, :],
                          scalar1=C["invs"][:, st:st + 1], scalar2=C["biasB"][:, st:st + 1],
                          op0=AluOpType.mult, op1=AluOpType.add),
          reads=[("xr", st % 2), ("cmB", p, st)],
          writes=[("xntok", st % 4)])

    def emit_prep_loads(b, t):
        emit_prep_xload(b, t)

    def emit_prep_compute(b, t):
        xf = xnfm[t % 2]
        for s_ in range(4):
            st = 4 * t + s_
            xt_ = xntok[st % 4]
            bank = tb[st % 2]
            for dc in range(DC):
                A("T", lambda e, xt_=xt_, bank=bank, dc=dc:
                  e.transpose(bank[:, dc * 128:(dc + 1) * 128], xt_[:, dc * 128:(dc + 1) * 128],
                              identbf[:, :]),
                  reads=[("xntok", st % 4), ("identbf",)], writes=[("ps_tb", st % 2)])
            A("V", lambda e, bank=bank, s_=s_, xf=xf:
              e.tensor_copy(xf.rearrange("p (c f) -> p c f", c=8)[:, :, s_ * 128:(s_ + 1) * 128],
                            bank[:, :].rearrange("p (c f) -> p c f", c=8)),
              reads=[("ps_tb", st % 2)], writes=[("xnfm", t % 2)])
        pf = pfm[t % 2]
        for rc in range(2):
            for dc in range(DC):
                A("T", lambda e, rc=rc, dc=dc, xf=xf:
                  e.matmul(sbank[rc][:, :], u1sb[:, dc * 256 + rc * 128: dc * 256 + (rc + 1) * 128],
                           xf[:, dc * 512:(dc + 1) * 512], start=(dc == 0), stop=(dc == DC - 1)),
                  reads=[("xnfm", t % 2), ("w", "u1")], writes=[("ps_s", rc)], tag=f"P{b}_{t}")
            A("V", lambda e, rc=rc, pf=pf: e.tensor_copy(pf[:, rc * 512:(rc + 1) * 512], sbank[rc][:, :]),
              reads=[("ps_s", rc)], writes=[("pfm", t % 2)])

    def emit_phase3_prep(b, t):
        emit_prep_loads(b, t)
        emit_prep_compute(b, t)

    def emit_phase3_main(b, t):
        p = b % 2
        xf = xnfm[t % 2]
        pf = pfm[t % 2]
        if t + 2 < NT and PREP_IN_MAIN:
            emit_prep_loads(b, t + 2)
        def emit_H(fc):
            hb = sbank[2 + (fc % 2)]
            A("T", lambda e, fc=fc, hb=hb:
              e.matmul(hb[:, :], v1sb[:, 0 * 4096 + fc * 128:0 * 4096 + (fc + 1) * 128],
                       pf[:, 0:512], start=True, stop=False),
              reads=[("pfm", t % 2), ("w", "v1")], writes=[("ps_s", 2 + (fc % 2))])
            A("T", lambda e, fc=fc, hb=hb:
              e.matmul(hb[:, :], v1sb[:, 1 * 4096 + fc * 128:1 * 4096 + (fc + 1) * 128],
                       pf[:, 512:1024], start=False, stop=True),
              reads=[("pfm", t % 2), ("w", "v1")], writes=[("ps_s", 2 + (fc % 2))])
            A("S", lambda e, fc=fc, hb=hb:
              e.activation(out=hbf[fc % 2][:, :], in_=hb[:, :], func=AFT.Gelu_apprx_tanh,
                           scale=1.0, bias=fb1c[:, fc:fc + 1]),
              reads=[("ps_s", 2 + (fc % 2)), ("fb1c",)], writes=[("hbf", fc % 2)])
        def emit_Q(fc):
            for rc in range(2):
                A("T", lambda e, fc=fc, rc=rc:
                  e.matmul([q0, q1][rc][:, :],
                           u2sb[:, fc * 256 + rc * 128: fc * 256 + (rc + 1) * 128],
                           hbf[fc % 2][:, :], start=(fc == 0), stop=(fc == 31)),
                  reads=[("hbf", fc % 2), ("w", "u2")],
                  writes=[("ps_q0",) if rc == 0 else ("ps_q1",)], tag=f"Q{b}_{t}")
        emit_H(0)
        for fc in range(1, 32):
            emit_H(fc)
            emit_Q(fc - 1)
        emit_Q(31)
        for rc in range(2):
            A("V", lambda e, rc=rc: e.tensor_copy(qfm[:, rc * 512:(rc + 1) * 512], [q0, q1][rc][:, :]),
              reads=[("ps_q0",) if rc == 0 else ("ps_q1",)], writes=[("qfm",)])
        if t + 2 < NT and CF_IN_MAIN:
            emit_cf_tile(b, t + 2)
        for s_ in range(4):
            st = 4 * t + s_
            pb = 2 * (s_ % 2)
            bufr = xr[st % 2]
            A("SY", lambda e, b=b, st=st, bufr=bufr:
              e.dma_start(out=bufr[:, :], in_=x[b, st * 128:(st + 1) * 128, :]),
              writes=[("xr", st % 2)], chan=("QXR", st % 2), dma=True)
            A("G", lambda e, bufr=bufr, st=st:
              e.tensor_tensor(out=xb[st % 2][:, :], in0=bufr[:, :], in1=b2b[:, :],
                              op=AluOpType.add),
              reads=[("xr", st % 2), ("b2b",)], writes=[("xb", st % 2), ("wstage", 0)])
            for h in range(2):
                bank = sbank[pb + h]
                for dc in range(DC):
                    A("T", lambda e, h=h, dc=dc, s_=s_, bank=bank, xf=xf:
                      e.matmul(bank[:, :], xf[:, dc * 512 + s_ * 128: dc * 512 + (s_ + 1) * 128],
                               w2sb[:, dc * 1024 + h * 512: dc * 1024 + h * 512 + 512],
                               start=(dc == 0), stop=False),
                      reads=[("xnfm", t % 2), ("w", "w2")], writes=[("ps_s", pb + h)], tag=f"F{b}_{t}_{s_}")
                for dc in range(DC):
                    A("T", lambda e, h=h, dc=dc, s_=s_, t=t, bank=bank:
                      e.matmul(bank[:, :],
                               nfm[:, dc * 2048 + t * 512 + s_ * 128: dc * 2048 + t * 512 + (s_ + 1) * 128],
                               w2sb[:, (8 + dc) * 1024 + h * 512: (8 + dc) * 1024 + h * 512 + 512],
                               start=False, stop=False),
                      reads=[("nfm", t), ("w", "w2")], writes=[("ps_s", pb + h)])
                for rc in range(2):
                    A("T", lambda e, h=h, rc=rc, s_=s_, bank=bank:
                      e.matmul(bank[:, :], qfm[:, rc * 512 + s_ * 128: rc * 512 + (s_ + 1) * 128],
                               v2sb[:, rc * 1024 + h * 512: rc * 1024 + h * 512 + 512],
                               start=False, stop=(rc == 1)),
                      reads=[("qfm",), ("w", "v2")], writes=[("ps_s", pb + h)])
                A("V", lambda e, h=h, st=st, bank=bank:
                  e.tensor_tensor(out=outsb[st % 3][:, h * 512:(h + 1) * 512], in0=bank[:, :],
                                  in1=xb[st % 2][:, h * 512:(h + 1) * 512], op=AluOpType.add),
                  reads=[("ps_s", pb + h), ("xb", st % 2)], writes=[("outsb", st % 3)])
            A("G", lambda e, b=b, st=st:
              e.dma_start(out=out[b, st * 128:(st + 1) * 128, :], in_=outsb[st % 3][:, :]),
              reads=[("outsb", st % 3)], chan=("QST", st % 3), dma=True)
        if t + 2 < NT and PREP_IN_MAIN:
            emit_prep_compute(b, t + 2)

    def emit_weights():
        chunks = []
        u1v = U1.rearrange("(c p) r -> p c r", p=128)
        chunks.append((lambda: u1v[:, :, :], u1sb, 0, 2048))
        v1v = V1.rearrange("(c p) f -> p c f", p=128)
        for j in range(4):
            c, hh = j // 2, j % 2
            chunks.append((lambda c=c, hh=hh: v1v[:, c:c + 1, hh * 2048:(hh + 1) * 2048],
                           v1sb, c * 4096 + hh * 2048, 2048))
        u2v = U2.rearrange("(c p) r -> p c r", p=128)
        for j in range(4):
            chunks.append((lambda j=j: u2v[:, 8 * j:8 * (j + 1), :], u2sb, j * 2048, 2048))
        v2v = V2.rearrange("(c p) d -> p c d", p=128)
        chunks.append((lambda: v2v[:, :, :], v2sb, 0, 2048))
        w2v = W2.rearrange("(c p) d -> p c d", p=128)
        for j in range(8):
            chunks.append((lambda j=j: w2v[:, 2 * j:2 * (j + 1), :], w2sb, j * 2048, 2048))
        names = {id(u1sb): "u1", id(v1sb): "v1", id(u2sb): "u2", id(v2sb): "v2", id(w2sb): "w2"}
        eng_of = {"w2": "V", "u1": "S", "v1": "S", "u2": "S", "v2": "S"}
        for i, (apf, dest, off, n) in enumerate(chunks):
            stg = wstage[i % 2]
            A("SY", lambda e, apf=apf, stg=stg: e.dma_start(out=stg[:, :], in_=apf()),
              writes=[("wstage", i % 2)], chan=("QW", i % 2), dma=True)
            wname = names[id(dest)]
            if eng_of[wname] == "V":
                A("V", lambda e, stg=stg, dest=dest, off=off, n=n:
                  e.tensor_copy(dest[:, off:off + n], stg[:, :]),
                  reads=[("wstage", i % 2)], writes=[("w", wname)])
            else:
                A("S", lambda e, stg=stg, dest=dest, off=off, n=n:
                  e.copy(dest[:, off:off + n], stg[:, :]),
                  reads=[("wstage", i % 2)], writes=[("w", wname)])

    for g in range(NT):
        emit_stats(0, [g], nslot=6, sum_on_v=True)
        emit_colmath(0, 4 * g, 4 * g + 4)
        emit_phase1_tile(0, g, nslot=6, cast_eng="S")
    emit_weights()
    for b in range(NB):
        nxt = b + 1
        for t in range(NT):
            emit_sim_tile(b, t)
            if nxt < NB:
                emit_stats(nxt, [t], nslot=6)
            if t == 1 and PREP_IN_MAIN:
                emit_prep_loads(b, 0)
        for t in range(2 if CF_IN_MAIN else NT):
            emit_cf_tile(b, t)
            if t < 2 and PREP_IN_MAIN:
                emit_prep_compute(b, t)
                if t == 0:
                    emit_prep_loads(b, 1)
        if not PREP_IN_MAIN:
            for t in range(2):
                emit_phase3_prep(b, t)
        if nxt < NB:
            emit_colmath(nxt, 0, 16)
        for t in range(NT):
            emit_phase3_main(b, t)
            if t + 2 < NT and not PREP_IN_MAIN:
                emit_phase3_prep(b, t + 2)
            if nxt < NB:
                emit_phase1_tile(nxt, t)

    sch.finalize()
    globals()["LAST_SCHED"] = sch
    with nc.Block() as block:
        @block.tensor
        def _(eng):
            sch.emit("T", eng, sems)

        @block.vector
        def _(eng):
            sch.emit("V", eng, sems)

        @block.scalar
        def _(eng):
            sch.emit("S", eng, sems)

        @block.gpsimd
        def _(eng):
            sch.emit("G", eng, sems)
            for k in range(3):
                n_store = sch.chan_count.get(("QST", k), 0)
                if n_store:
                    eng.wait_ge(sems[("QST", k)], 16 * n_store)

        @block.sync
        def _(eng):
            sch.emit("SY", eng, sems)
    es.close()
    return nc


def kernel(L0_attn_data, L0_attn_W1, L0_attn_b1, L0_attn_W2,
           L0_ffn_U1, L0_ffn_V1, L0_ffn_b1, L0_ffn_U2, L0_ffn_V2, L0_ffn_b2):
    n = 8
    def h(a):
        a = np.asarray(a)
        assert a.dtype == np.float32, a.dtype
        return a
    xfull = np.ascontiguousarray(h(L0_attn_data))
    com = {
        "W1": np.ascontiguousarray(h(L0_attn_W1).reshape(8, 128).T),
        "b1": np.ascontiguousarray(h(L0_attn_b1).reshape(8, 128).T),
        "W2": np.ascontiguousarray(h(L0_attn_W2)),
        "U1": np.ascontiguousarray(h(L0_ffn_U1)),
        "V1": np.ascontiguousarray(h(L0_ffn_V1)),
        "fb1": np.ascontiguousarray(h(L0_ffn_b1).reshape(32, 128).T),
        "U2": np.ascontiguousarray(h(L0_ffn_U2)),
        "V2": np.ascontiguousarray(h(L0_ffn_V2)),
        "b2": np.ascontiguousarray(h(L0_ffn_b2).reshape(1, 1024)),
        "idb": np.eye(128, dtype=np.float32).astype(ml_dtypes.bfloat16),
        "idf": np.eye(128, dtype=np.float32),
    }
    nc = build_program()
    in_maps = [dict(com, x=np.ascontiguousarray(xfull[2 * c:2 * c + 2])) for c in range(n)]
    res = run_bass_kernel_spmd(nc, in_maps, core_ids=list(range(n)))
    return np.concatenate([res.results[c]["out"] for c in range(n)], axis=0)


# version 35
# speedup vs baseline: 1.0067x; 1.0067x over previous
import numpy as np
import ml_dtypes
import concourse.bass as bass
import concourse.mybir as mybir
from concourse.bass_utils import run_bass_kernel_spmd
from concourse.alu_op_type import AluOpType

F32 = mybir.dt.float32
BF16 = mybir.dt.bfloat16
AFT = mybir.ActivationFunctionType
AX = mybir.AxisListType

CF_IN_MAIN = True
PREP_IN_MAIN = True
NB = 2
SEQ = 2048
D = 1024
DC = 8
NT = 4
NQ = 16

class _Op:
    __slots__ = ("stream", "chan", "fn", "dma", "pos", "chan_idx", "marked",
                 "waits", "drain", "clock", "tag")

class Sched:
    def __init__(self, streams, total_chans=()):
        self.streams = {s: [] for s in streams}
        self.total_chans = set(total_chans)
        self.chan_count = {}
        self.res = {}
        self.stream_clock = {s: {} for s in streams}
        self.last_drain = {s: -1 for s in streams}

    def add(self, stream, fn, reads=(), writes=(), chan=None, dma=False, tag=""):
        chan = chan or stream
        op = _Op()
        op.stream, op.chan, op.fn, op.dma, op.tag = stream, chan, fn, dma, tag
        op.pos = len(self.streams[stream])
        op.chan_idx = self.chan_count.get(chan, 0)
        self.chan_count[chan] = op.chan_idx + 1
        op.marked = dma
        op.waits = {}
        op.drain = False
        deps = []
        for k in reads:
            st = self.res.get(k)
            if st and st[0] is not None:
                deps.append((st[0], False))
        for k in writes:
            st = self.res.get(k)
            if st:
                if st[0] is not None:
                    deps.append((st[0], False))
                for r in st[1].values():
                    deps.append((r, True))
        clock = dict(self.stream_clock[stream])
        for dep, is_war in deps:
            if dep is op:
                continue
            if dep.dma:
                if dep.chan in self.total_chans:
                    idx = self.chan_count[dep.chan] - 1
                else:
                    idx = dep.chan_idx
                if clock.get(dep.chan, -1) < idx:
                    op.waits[dep.chan] = max(op.waits.get(dep.chan, -1), idx)
                    for c2, i2 in dep.clock.items():
                        if clock.get(c2, -1) < i2:
                            clock[c2] = i2
                    clock[dep.chan] = idx
            elif dep.stream == stream:
                if not is_war and stream != "T":
                    if clock.get(dep.chan, -1) < dep.chan_idx:
                        dep.marked = True
                        op.waits[dep.chan] = max(op.waits.get(dep.chan, -1), dep.chan_idx)
                        clock[dep.chan] = dep.chan_idx
            else:
                if clock.get(dep.chan, -1) < dep.chan_idx:
                    dep.marked = True
                    op.waits[dep.chan] = max(op.waits.get(dep.chan, -1), dep.chan_idx)
                    for c2, i2 in dep.clock.items():
                        if clock.get(c2, -1) < i2:
                            clock[c2] = i2
                    clock[dep.chan] = dep.chan_idx
        op.clock = clock
        self.stream_clock[stream] = clock
        for k in reads:
            self.res.setdefault(k, [None, {}])[1][stream] = op
        for k in writes:
            self.res[k] = [op, {}]
        self.streams[stream].append(op)
        return op

    def finalize(self):
        self.mark_val = {}
        per_chan_ops = {}
        for s, ops in self.streams.items():
            for o in ops:
                per_chan_ops.setdefault(o.chan, []).append(o)
        for c, ops in per_chan_ops.items():
            ops.sort(key=lambda o: o.chan_idx)
            acc = 0
            vals = []
            for o in ops:
                if o.marked:
                    acc += 16 if o.dma else 1
                vals.append(acc)
            self.mark_val[c] = vals

    def wait_value(self, chan, idx):
        return self.mark_val[chan][idx]

    def emit(self, stream, eng, sems):
        for o in self.streams[stream]:
            if o.drain:
                eng.drain()
            for c, idx in sorted(o.waits.items(), key=lambda kv: str(kv[0])):
                v = self.wait_value(c, idx)
                if v > 0:
                    eng.wait_ge(sems[c], v)
            inst = o.fn(eng)
            if o.marked:
                if inst is None:
                    inst = eng.nop()
                inst.then_inc(sems[o.chan], 16 if o.dma else 1)


def build_program():
    nc = bass.Bass("TRN2", target_bir_lowering=False)
    from contextlib import ExitStack
    es = ExitStack()

    x = nc.dram_tensor("x", [NB, SEQ, D], F32, kind="ExternalInput")
    W1 = nc.dram_tensor("W1", [128, 8], F32, kind="ExternalInput")
    b1 = nc.dram_tensor("b1", [128, 8], F32, kind="ExternalInput")
    W2 = nc.dram_tensor("W2", [2 * D, D], F32, kind="ExternalInput")
    U1 = nc.dram_tensor("U1", [D, 256], F32, kind="ExternalInput")
    V1 = nc.dram_tensor("V1", [256, 4096], F32, kind="ExternalInput")
    fb1 = nc.dram_tensor("fb1", [128, 32], F32, kind="ExternalInput")
    U2 = nc.dram_tensor("U2", [4096, 256], F32, kind="ExternalInput")
    V2 = nc.dram_tensor("V2", [256, D], F32, kind="ExternalInput")
    b2 = nc.dram_tensor("b2", [1, D], F32, kind="ExternalInput")
    idb = nc.dram_tensor("idb", [128, 128], BF16, kind="ExternalInput")
    idf = nc.dram_tensor("idf", [128, 128], F32, kind="ExternalInput")
    out = nc.dram_tensor("out", [NB, SEQ, D], F32, kind="ExternalOutput")

    def sb(name, shape, dt=F32):
        return es.enter_context(nc.sbuf_tensor(name, shape, dt))

    def ps(name, shape, dt=F32):
        return es.enter_context(nc.psum_tensor(name, shape, dt))

    w2sb = sb("w2sb", [128, 16 * 1024], BF16)
    u1sb = sb("u1sb", [128, 8 * 256], BF16)
    v1sb = sb("v1sb", [128, 2 * 4096], BF16)
    u2sb = sb("u2sb", [128, 32 * 256], BF16)
    v2sb = sb("v2sb", [128, 2 * 1024], BF16)
    wstage = [sb(f"wstage{i}", [128, 2048]) for i in range(2)]
    w1c = sb("w1c", [128, 8])
    b1c = sb("b1c", [128, 8])
    fb1c = sb("fb1c", [128, 32])
    cscr = sb("cscr", [1, 512])
    b2b = sb("b2b", [128, 1024])
    onesf = sb("onesf", [1, 128])
    onesel = sb("onesel", [128, 16], BF16)
    identbf = sb("identbf", [128, 128], BF16)
    identf = sb("identf", [128, 128])

    nfm = sb("nfm", [128, 8 * 2048], BF16)
    xa = [sb(f"xa{i}", [128, 1024]) for i in range(2)]
    xr = [sb(f"xr{i}", [128, 1024]) for i in range(2)]
    xb = [wstage[0][:, 0:1024], wstage[0][:, 1024:2048]]
    outsb = [sb(f"outsb{i}", [128, 1024]) for i in range(3)]
    xnfm = [sb(f"xnfm{i}", [128, 8 * 512], BF16) for i in range(2)]
    ntok = [sb(f"ntok{i}", [128, 1024], BF16) for i in range(2)]
    xntok = [sb(f"xntok{i}", [128, 1024], BF16) for i in range(4)]
    pfm = [sb(f"pfm{i}", [128, 2 * 512], BF16) for i in range(2)]
    qfm = sb("qfm", [128, 2 * 512], BF16)
    hbf = [sb(f"hbf{i}", [128, 512], BF16) for i in range(2)]
    e512 = [wstage[1][:, 0:512], wstage[1][:, 512:1024]]
    vjunk = sb("vjunk", [128, 2048], BF16)
    cpart = [sb(f"cpart{i}", [128, 4]) for i in range(2)]
    call = sb("call", [128, 16])
    crow4 = [sb(f"crow4{i}", [4, 128]) for i in range(2)]
    crow1 = sb("crow1", [1, 2048])
    cols = {}
    for n in ["sum", "ssq", "mean", "var", "u", "ru", "sq", "rs", "rs2",
              "scaleA", "biasA", "invs", "biasB", "t1", "t2"]:
        cols[n] = [sb(f"c_{n}{p}", [128, 16]) for p in range(2)]
    sq32 = [sb(f"sq32_{p}", [128, 32]) for p in range(2)]
    iv32 = [sb(f"iv32_{p}", [128, 32]) for p in range(2)]

    s0 = ps("s0", [128, 512]); s1 = ps("s1", [128, 512])
    s2 = ps("s2", [128, 512]); s3 = ps("s3", [128, 512])
    q0 = ps("q0", [128, 512]); q1 = ps("q1", [128, 512])
    tb = [ps("tb0", [128, 1024], BF16), ps("tb1", [128, 1024], BF16)]
    sbank = [s0, s1, s2, s3]

    streams = ["T", "V", "S", "G", "SY"]
    dma_chans = (["QC"] + [("QW", i) for i in range(2)] + [("QXA", i) for i in range(2)]
                 + [("QXR", i) for i in range(2)] + [("QXO", i) for i in range(2)]
                 + [("QXO", 2)] + [("QTN", i) for i in range(2)]
                 + [("QST", i) for i in range(3)])
    sch = Sched(streams, total_chans=["QC"])
    sems = {}
    for c in streams + dma_chans:
        nm = c if isinstance(c, str) else c[0] + str(c[1])
        sems[c] = es.enter_context(nc.semaphore("sem_" + nm))

    A = sch.add

    K_nfm = [("nfm", t) for t in range(NT)]
    K_crow1 = [("crow1", t) for t in range(NT)]
    K_w = "wALL"
    A("SY", lambda e: e.dma_start(out=identbf[:, :], in_=idb[:, :]),
      writes=[("identbf",)], chan="QC", dma=True)
    A("SY", lambda e: e.dma_start(out=identf[:, :], in_=idf[:, :]),
      writes=[("identf",)], chan="QC", dma=True)
    A("SY", lambda e: e.dma_start(out=w1c[:, :], in_=W1[:, :]),
      writes=[("w1c",)], chan="QC", dma=True)
    A("SY", lambda e: e.dma_start(out=b1c[:, :], in_=b1[:, :]),
      writes=[("b1c",)], chan="QC", dma=True)
    A("SY", lambda e: e.dma_start(out=fb1c[:, :], in_=fb1[:, :]),
      writes=[("fb1c",)], chan="QC", dma=True)
    b2bc, _ = bass.broadcast_tensor_aps(b2[0:1, :], b2b[:, :])
    A("SY", lambda e: e.dma_start(out=b2b[:, :], in_=b2bc),
      writes=[("b2b",)], chan="QC", dma=True)
    A("V", lambda e: e.memset(onesf[:, :], 1.0), writes=[("onesf",)])
    A("V", lambda e: e.memset(onesel[:, :], 0.0), writes=[("onesel",)])
    for kc in range(4):
        A("V", lambda e, kc=kc: e.memset(onesel[:, 4 * kc + kc:4 * kc + kc + 1], 1.0),
          reads=[("onesel",)], writes=[("onesel",)])

    def xslot(st, nslot):
        i = st % nslot
        if i < 2:
            return xa[i], ("xa", i), ("QXA", i)
        if i < 4:
            return xr[i - 2], ("xr", i - 2), ("QXR", i - 2)
        return outsb[i - 4], ("outsb", i - 4), ("QXO", i - 4)

    def emit_stats(b, tiles, nslot=4, fused=False, sum_on_v=False):
        p = b % 2
        for t in tiles:
            for s_ in range(4):
                st = 4 * t + s_
                buf, bkey, bchan = xslot(st, nslot)
                A("SY", lambda e, b=b, st=st, buf=buf:
                  e.dma_start(out=buf[:, :], in_=x[b, st * 128:(st + 1) * 128, :]),
                  reads=[], writes=[bkey], chan=bchan, dma=True)
                if sum_on_v:
                    A("V", lambda e, buf=buf, p=p, st=st:
                      e.tensor_scalar(out=vjunk[:, 0:1024], in0=buf[:, :], scalar1=0.0,
                                      scalar2=None, op0=AluOpType.add, op1=AluOpType.add,
                                      accum_out=cols["sum"][p][:, st:st + 1]),
                      reads=[bkey], writes=[("c_sum", p, st), ("vj", 0), ("vj", 1)])
                else:
                    A("S", lambda e, buf=buf, p=p, st=st:
                      e.activation(out=outsb[2][:, :], in_=buf[:, :], func=AFT.Identity,
                                   accum_out=cols["sum"][p][:, st:st + 1]),
                      reads=[bkey], writes=[("c_sum", p, st), ("outsb", 2)])
                if b == 0:
                    A("S", lambda e, buf=buf, p=p, st=st:
                      e.activation(out=vjunk[:, 1024:2048], in_=buf[:, :], func=AFT.Square,
                                   accum_out=cols["ssq"][p][:, st:st + 1]),
                      reads=[bkey], writes=[("c_ssq", p, st), ("vj", 2), ("vj", 3)])
                else:
                    A("S", lambda e, buf=buf, p=p, st=st:
                      e.activation(out=outsb[2][:, :], in_=buf[:, :], func=AFT.Square,
                                   accum_out=cols["ssq"][p][:, st:st + 1]),
                      reads=[bkey], writes=[("c_ssq", p, st), ("outsb", 2)])
            if fused:
                emit_colmath(b, 4 * t, 4 * t + 4)
                C = {n: cols[n][p] for n in cols}
                for s_ in range(4):
                    st = 4 * t + s_
                    buf, bkey, bchan = xslot(st, nslot)
                    nt_ = ntok[st % 2]
                    A("V", lambda e, buf=buf, nt_=nt_, st=st, C=C:
                      e.tensor_scalar(out=nt_[:, :], in0=buf[:, :],
                                      scalar1=C["scaleA"][:, st:st + 1],
                                      scalar2=C["biasA"][:, st:st + 1],
                                      op0=AluOpType.mult, op1=AluOpType.add),
                      reads=[bkey, ("cmA", p, st)], writes=[("ntok", st % 2)])
                    bank = tb[st % 2]
                    for dc in range(DC):
                        A("T", lambda e, nt_=nt_, bank=bank, dc=dc:
                          e.transpose(bank[:, dc * 128:(dc + 1) * 128],
                                      nt_[:, dc * 128:(dc + 1) * 128], identbf[:, :]),
                          reads=[("ntok", st % 2), ("identbf",)], writes=[("ps_tb", st % 2)])
                    A("V", lambda e, bank=bank, t=t, s_=s_:
                      e.tensor_copy(nfm.rearrange("p (c f) -> p c f", c=8)[:, :, t * 512 + s_ * 128: t * 512 + (s_ + 1) * 128],
                                    bank[:, :].rearrange("p (c f) -> p c f", c=8)),
                      reads=[("ps_tb", st % 2)], writes=[("nfm", t)])

    def emit_colmath(b, s0, s1):
        p = b % 2
        C = {n: cols[n][p][:, s0:s1] for n in cols}
        K = lambda n: ("cm", p, n, s0)
        rd_sum = [("c_sum", p, st) for st in range(s0, s1)]
        rd_ssq = [("c_ssq", p, st) for st in range(s0, s1)]
        def vop(fn, reads, writes):
            A("V", fn, reads=reads, writes=writes)
        vop(lambda e: e.tensor_scalar(out=C["mean"], in0=C["sum"],
            scalar1=1.0 / D, scalar2=None, op0=AluOpType.mult), rd_sum, [K("mean")])
        vop(lambda e: e.tensor_scalar(out=C["t1"], in0=C["ssq"],
            scalar1=1.0 / D, scalar2=None, op0=AluOpType.mult), rd_ssq, [K("t1")])
        vop(lambda e: e.tensor_tensor(out=C["t2"], in0=C["mean"],
            in1=C["mean"], op=AluOpType.mult), [K("mean")], [K("t2")])
        vop(lambda e: e.tensor_tensor(out=C["var"], in0=C["t1"],
            in1=C["t2"], op=AluOpType.subtract), [K("t1"), K("t2")], [K("var")])
        vop(lambda e: e.tensor_scalar(out=C["u"], in0=C["var"],
            scalar1=1e-5, scalar2=None, op0=AluOpType.add), [K("var")], [K("u")])
        vop(lambda e: e.reciprocal(C["ru"], C["u"]), [K("u")], [K("ru")])
        vop(lambda e: e.tensor_tensor(out=C["t1"], in0=C["var"],
            in1=C["ru"], op=AluOpType.mult), [K("var"), K("ru")], [K("t1")])
        vop(lambda e: e.tensor_scalar(out=C["sq"], in0=C["t1"],
            scalar1=float(D), scalar2=1e-12, op0=AluOpType.mult, op1=AluOpType.max),
            [K("t1")], [K("sq")])
        vop(lambda e: e.reciprocal(C["rs"], C["sq"]), [K("sq")], [K("rs")])
        sqa = sq32[p][:, s0:s1]; sqb = sq32[p][:, 16 + s0:16 + s1]
        iva = iv32[p][:, s0:s1]; ivb = iv32[p][:, 16 + s0:16 + s1]
        vop(lambda e: e.tensor_copy(sqa, C["ru"]), [K("ru")], [K("s32a")])
        vop(lambda e: e.tensor_copy(sqb, C["rs"]), [K("rs")], [K("s32b")])
        A("S", lambda e: e.activation(out=iva, in_=sqa, func=AFT.Sqrt),
          reads=[K("s32a")], writes=[K("iva")])
        A("S", lambda e: e.activation(out=ivb, in_=sqb, func=AFT.Sqrt),
          reads=[K("s32b")], writes=[K("ivb")])
        def newton(ycols, ykey, acol, akey, t1, outc, outk):
            vop(lambda e: e.tensor_tensor(out=t1, in0=ycols, in1=ycols,
                op=AluOpType.mult), [ykey], [K(outk + "t1")])
            vop(lambda e: e.tensor_tensor(out=t1, in0=t1, in1=acol,
                op=AluOpType.mult), [K(outk + "t1"), akey], [K(outk + "t1")])
            vop(lambda e: e.tensor_scalar(out=t1, in0=t1, scalar1=-0.5,
                scalar2=1.5, op0=AluOpType.mult, op1=AluOpType.add),
                [K(outk + "t1")], [K(outk + "t1")])
            vop(lambda e: e.tensor_tensor(out=outc, in0=ycols, in1=t1,
                op=AluOpType.mult), [ykey, K(outk + "t1")], [K(outk)])
        newton(iva, K("iva"), C["u"], K("u"), C["t1"], C["invs"], "invs")
        newton(ivb, K("ivb"), C["sq"], K("sq"), C["t2"], C["rs2"], "rs2")
        vop(lambda e: e.tensor_tensor(out=C["scaleA"], in0=C["invs"],
            in1=C["rs2"], op=AluOpType.mult),
            [K("invs"), K("rs2")], [K("scaleA")])
        vop(lambda e: e.scalar_tensor_tensor(out=C["biasA"], in0=C["mean"],
            scalar=-1.0, in1=C["scaleA"], op0=AluOpType.mult, op1=AluOpType.mult),
            [K("mean"), K("scaleA")], [K("biasA")] + [("cmA", p, st) for st in range(s0, s1)])
        vop(lambda e: e.scalar_tensor_tensor(out=C["biasB"], in0=C["mean"],
            scalar=-1.0, in1=C["invs"], op0=AluOpType.mult, op1=AluOpType.mult),
            [K("mean"), K("invs")], [K("biasB")] + [("cmB", p, st) for st in range(s0, s1)])

    def emit_phase1_tile(b, t, nslot=2, cast_eng="V"):
        p = b % 2
        C = {n: cols[n][p] for n in cols}
        for s_ in range(4):
            st = 4 * t + s_
            buf, bkey, bchan = xslot(st, nslot)
            A("SY", lambda e, b=b, st=st, buf=buf:
              e.dma_start(out=buf[:, :], in_=x[b, st * 128:(st + 1) * 128, :]),
              writes=[bkey], chan=bchan, dma=True)
            nt_ = ntok[st % 2]
            if cast_eng == "V":
                A("V", lambda e, buf=buf, nt_=nt_, st=st:
                  e.tensor_scalar(out=nt_[:, :], in0=buf[:, :],
                                  scalar1=C["scaleA"][:, st:st + 1], scalar2=C["biasA"][:, st:st + 1],
                                  op0=AluOpType.mult, op1=AluOpType.add),
                  reads=[bkey, ("cmA", p, st)],
                  writes=[("ntok", st % 2)])
            else:
                A("S", lambda e, buf=buf, nt_=nt_, st=st:
                  e.activation(out=nt_[:, :], in_=buf[:, :], func=AFT.Identity,
                               scale=C["scaleA"][:, st:st + 1], bias=C["biasA"][:, st:st + 1]),
                  reads=[bkey, ("cmA", p, st)],
                  writes=[("ntok", st % 2)])
            bank = tb[st % 2]
            for dc in range(DC):
                A("T", lambda e, nt_=nt_, bank=bank, dc=dc:
                  e.transpose(bank[:, dc * 128:(dc + 1) * 128], nt_[:, dc * 128:(dc + 1) * 128],
                              identbf[:, :]),
                  reads=[("ntok", st % 2), ("identbf",)], writes=[("ps_tb", st % 2)])
            if cast_eng == "S":
                A("V", lambda e, bank=bank, t=t, s_=s_:
                  e.tensor_copy(nfm.rearrange("p (c f) -> p c f", c=8)[:, :, t * 512 + s_ * 128: t * 512 + (s_ + 1) * 128],
                                bank[:, :].rearrange("p (c f) -> p c f", c=8)),
                  reads=[("ps_tb", st % 2)], writes=[("nfm", t)], tag=f"nfmev{b}_{t}{s_}")
            else:
                A("S", lambda e, bank=bank, t=t, s_=s_:
                  e.copy(nfm.rearrange("p (c f) -> p c f", c=8)[:, :, t * 512 + s_ * 128: t * 512 + (s_ + 1) * 128],
                         bank[:, :].rearrange("p (c f) -> p c f", c=8)),
                  reads=[("ps_tb", st % 2)], writes=[("nfm", t)], tag=f"nfmev{b}_{t}{s_}")

    sim_state = {}

    def emit_sim_tile(b, t):
        if t == 0:
            sim_state["bc"] = 0
            sim_state["cs"] = 0
            sim_state["pending"] = []
            sim_state["ncs"] = sum(3 - i // 4 for i in range(16))
        def flush_pending():
            for (vslot, kc) in sim_state["pending"]:
                ci = sim_state["cs"]; sim_state["cs"] += 1
                A("T", lambda e, vslot=vslot, kc=kc, ci=ci:
                  e.matmul(q1[0:4, 0:512], onesel[:, 4 * kc:4 * kc + 4],
                           vjunk[:, vslot * 512:(vslot + 1) * 512],
                           start=(ci == 0), stop=(ci == sim_state["ncs"] - 1)),
                  reads=[("vj", vslot), ("onesel",)], writes=[("ps_q1",)], tag=f"cs{b}")
            sim_state["pending"] = []
        for qq in range(4):
            q = 4 * t + qq
            cp = cpart[q % 2]
            nb = 4 - q // 4
            for j, kc in enumerate(range(q // 4, 4)):
                bank = sbank[sim_state["bc"] % 4]
                bkkey = ("ps_s", sim_state["bc"] % 4)
                vslot = sim_state["bc"] % 4
                for dc in range(DC):
                    A("T", lambda e, bank=bank, dc=dc, q=q, kc=kc:
                      e.matmul(bank[:, :],
                               nfm[:, dc * 2048 + q * 128: dc * 2048 + (q + 1) * 128],
                               nfm[:, dc * 2048 + kc * 512: dc * 2048 + (kc + 1) * 512],
                               start=(dc == 0), stop=(dc == DC - 1)),
                      reads=[("nfm", tt) for tt in range(NT)],
                      writes=[bkkey], tag=f"sim{b}_{q}_{kc}")
                flush_pending()
                A("V", lambda e, bank=bank, cp=cp, j=j, vslot=vslot:
                  e.tensor_scalar(out=vjunk[:, vslot * 512:(vslot + 1) * 512],
                                  in0=bank[:, :], scalar1=0.0,
                                  scalar2=None, op0=AluOpType.max, op1=AluOpType.add,
                                  accum_out=cp[:, j:j + 1]),
                  reads=[bkkey], writes=[("cpart", q % 2, j), ("vj", vslot)])
                if kc > q // 4:
                    sim_state["pending"].append((vslot, kc))
                sim_state["bc"] += 1
            A("V", lambda e, cp=cp, q=q, nb=nb:
              e.reduce_sum(call[:, q:q + 1], cp[:, 0:nb], axis=AX.X),
              reads=[("cpart", q % 2, j) for j in range(nb)], writes=[("call", t)])
        c4 = crow4[t % 2]
        A("T", lambda e, t=t: e.transpose(q0[0:4, 0:128], call[:, 4 * t:4 * t + 4], identf[:, :]),
          reads=[("call", t), ("identf",)], writes=[("ps_q0",)])
        A("V", lambda e, c4=c4: e.tensor_copy(c4[:, :], q0[0:4, 0:128]),
          reads=[("ps_q0",)], writes=[("crow4", t % 2)])
        for qi in range(4):
            A("SY", lambda e, c4=c4, t=t, qi=qi:
              e.dma_start(out=crow1[0:1, t * 512 + qi * 128: t * 512 + (qi + 1) * 128],
                          in_=c4[qi:qi + 1, :]),
              reads=[("crow4", t % 2)], writes=[("crow1", t)], chan=("QTN", t % 2), dma=True)
        if t == NT - 1:
            flush_pending()
            A("V", lambda e: e.tensor_copy(wstage[1][0:4, 1024:1536], q1[0:4, 0:512]),
              reads=[("ps_q1",)], writes=[("colsb",), ("wstage", 1)])
            for kc in range(1, 4):
                A("SY", lambda e, kc=kc:
                  e.dma_start(out=cscr[0:1, 0:512], in_=wstage[1][kc:kc + 1, 1024:1536]),
                  reads=[("colsb",)], writes=[("cscr",)], chan=("QTN", 0), dma=True)
                A("V", lambda e, kc=kc:
                  e.tensor_tensor(out=crow1[0:1, kc * 512:(kc + 1) * 512],
                                  in0=crow1[0:1, kc * 512:(kc + 1) * 512],
                                  in1=cscr[0:1, 0:512], op=AluOpType.add),
                  reads=[("cscr",), ("crow1", kc)], writes=[("crow1", kc)])

    def emit_cf_tile(b, t):
        qb = [q0, q1][t % 2]
        qk = ("ps_q0",) if t % 2 == 0 else ("ps_q1",)
        A("T", lambda e, t=t, qb=qb: e.matmul(qb[:, :], onesf[:, :],
                                       crow1[0:1, t * 512:(t + 1) * 512], start=True, stop=True),
          reads=[("onesf",), ("crow1", t)], writes=[qk], tag=f"T2_{b}_{t}")
        for dc in range(DC):
            ebuf = e512[dc % 2]
            A("S", lambda e, ebuf=ebuf, dc=dc, qb=qb:
              e.activation(out=ebuf[:, :], in_=qb[:, :], func=AFT.Exp,
                           scale=w1c[:, dc:dc + 1], bias=b1c[:, dc:dc + 1]),
              reads=[qk, ("w1c",), ("b1c",)], writes=[("e512", dc % 2), ("wstage", 1)])
            A("S", lambda e, ebuf=ebuf, dc=dc, t=t:
              e.activation(out=nfm[:, dc * 2048 + t * 512: dc * 2048 + (t + 1) * 512],
                           in_=ebuf[:, :], func=AFT.Ln, bias=1.0, scale=1.0),
              reads=[("e512", dc % 2)], writes=[("nfm", t)], tag=f"cf{b}_{t}_{dc}")

    def emit_prep_xload(b, t):
        for s_ in range(4):
            st = 4 * t + s_
            bufr = xr[st % 2]
            A("SY", lambda e, b=b, st=st, bufr=bufr:
              e.dma_start(out=bufr[:, :], in_=x[b, st * 128:(st + 1) * 128, :]),
              writes=[("xr", st % 2)], chan=("QXR", st % 2), dma=True)
            emit_prep_cast(b, st)

    def emit_prep_cast(b, st):
        p = b % 2
        C = {n: cols[n][p] for n in cols}
        bufr = xr[st % 2]
        xt_ = xntok[st % 4]
        A("V", lambda e, bufr=bufr, xt_=xt_, st=st:
          e.tensor_scalar(out=xt_[:, :], in0=bufr[:, :],
                          scalar1=C["invs"][:, st:st + 1], scalar2=C["biasB"][:, st:st + 1],
                          op0=AluOpType.mult, op1=AluOpType.add),
          reads=[("xr", st % 2), ("cmB", p, st)],
          writes=[("xntok", st % 4)])

    def emit_prep_loads(b, t):
        emit_prep_xload(b, t)

    def emit_prep_compute(b, t):
        xf = xnfm[t % 2]
        for s_ in range(4):
            st = 4 * t + s_
            xt_ = xntok[st % 4]
            bank = tb[st % 2]
            for dc in range(DC):
                A("T", lambda e, xt_=xt_, bank=bank, dc=dc:
                  e.transpose(bank[:, dc * 128:(dc + 1) * 128], xt_[:, dc * 128:(dc + 1) * 128],
                              identbf[:, :]),
                  reads=[("xntok", st % 4), ("identbf",)], writes=[("ps_tb", st % 2)])
            A("V", lambda e, bank=bank, s_=s_, xf=xf:
              e.tensor_copy(xf.rearrange("p (c f) -> p c f", c=8)[:, :, s_ * 128:(s_ + 1) * 128],
                            bank[:, :].rearrange("p (c f) -> p c f", c=8)),
              reads=[("ps_tb", st % 2)], writes=[("xnfm", t % 2)])
        pf = pfm[t % 2]
        for rc in range(2):
            for dc in range(DC):
                A("T", lambda e, rc=rc, dc=dc, xf=xf:
                  e.matmul(sbank[rc][:, :], u1sb[:, dc * 256 + rc * 128: dc * 256 + (rc + 1) * 128],
                           xf[:, dc * 512:(dc + 1) * 512], start=(dc == 0), stop=(dc == DC - 1)),
                  reads=[("xnfm", t % 2), ("w", "u1")], writes=[("ps_s", rc)], tag=f"P{b}_{t}")
            A("V", lambda e, rc=rc, pf=pf: e.tensor_copy(pf[:, rc * 512:(rc + 1) * 512], sbank[rc][:, :]),
              reads=[("ps_s", rc)], writes=[("pfm", t % 2)])

    def emit_phase3_prep(b, t):
        emit_prep_loads(b, t)
        emit_prep_compute(b, t)

    def emit_phase3_main(b, t):
        p = b % 2
        xf = xnfm[t % 2]
        pf = pfm[t % 2]
        if t + 2 < NT and PREP_IN_MAIN:
            emit_prep_loads(b, t + 2)
        def emit_H(fc):
            hb = sbank[2 + (fc % 2)]
            A("T", lambda e, fc=fc, hb=hb:
              e.matmul(hb[:, :], v1sb[:, 0 * 4096 + fc * 128:0 * 4096 + (fc + 1) * 128],
                       pf[:, 0:512], start=True, stop=False),
              reads=[("pfm", t % 2), ("w", "v1")], writes=[("ps_s", 2 + (fc % 2))])
            A("T", lambda e, fc=fc, hb=hb:
              e.matmul(hb[:, :], v1sb[:, 1 * 4096 + fc * 128:1 * 4096 + (fc + 1) * 128],
                       pf[:, 512:1024], start=False, stop=True),
              reads=[("pfm", t % 2), ("w", "v1")], writes=[("ps_s", 2 + (fc % 2))])
            A("S", lambda e, fc=fc, hb=hb:
              e.activation(out=hbf[fc % 2][:, :], in_=hb[:, :], func=AFT.Gelu_apprx_tanh,
                           scale=1.0, bias=fb1c[:, fc:fc + 1]),
              reads=[("ps_s", 2 + (fc % 2)), ("fb1c",)], writes=[("hbf", fc % 2)])
        def emit_Q(fc):
            for rc in range(2):
                A("T", lambda e, fc=fc, rc=rc:
                  e.matmul([q0, q1][rc][:, :],
                           u2sb[:, fc * 256 + rc * 128: fc * 256 + (rc + 1) * 128],
                           hbf[fc % 2][:, :], start=(fc == 0), stop=(fc == 31)),
                  reads=[("hbf", fc % 2), ("w", "u2")],
                  writes=[("ps_q0",) if rc == 0 else ("ps_q1",)], tag=f"Q{b}_{t}")
        emit_H(0)
        for fc in range(1, 32):
            emit_H(fc)
            emit_Q(fc - 1)
        emit_Q(31)
        for rc in range(2):
            A("V", lambda e, rc=rc: e.tensor_copy(qfm[:, rc * 512:(rc + 1) * 512], [q0, q1][rc][:, :]),
              reads=[("ps_q0",) if rc == 0 else ("ps_q1",)], writes=[("qfm",)])
        if t + 2 < NT and CF_IN_MAIN:
            emit_cf_tile(b, t + 2)
        for s_ in range(4):
            st = 4 * t + s_
            pb = 2 * (s_ % 2)
            bufr = xr[st % 2]
            A("SY", lambda e, b=b, st=st, bufr=bufr:
              e.dma_start(out=bufr[:, :], in_=x[b, st * 128:(st + 1) * 128, :]),
              writes=[("xr", st % 2)], chan=("QXR", st % 2), dma=True)
            A("G", lambda e, bufr=bufr, st=st:
              e.tensor_tensor(out=xb[st % 2][:, :], in0=bufr[:, :], in1=b2b[:, :],
                              op=AluOpType.add),
              reads=[("xr", st % 2), ("b2b",)], writes=[("xb", st % 2), ("wstage", 0)])
            for h in range(2):
                bank = sbank[pb + h]
                for dc in range(DC):
                    A("T", lambda e, h=h, dc=dc, s_=s_, bank=bank, xf=xf:
                      e.matmul(bank[:, :], xf[:, dc * 512 + s_ * 128: dc * 512 + (s_ + 1) * 128],
                               w2sb[:, dc * 1024 + h * 512: dc * 1024 + h * 512 + 512],
                               start=(dc == 0), stop=False),
                      reads=[("xnfm", t % 2), ("w", "w2")], writes=[("ps_s", pb + h)], tag=f"F{b}_{t}_{s_}")
                for dc in range(DC):
                    A("T", lambda e, h=h, dc=dc, s_=s_, t=t, bank=bank:
                      e.matmul(bank[:, :],
                               nfm[:, dc * 2048 + t * 512 + s_ * 128: dc * 2048 + t * 512 + (s_ + 1) * 128],
                               w2sb[:, (8 + dc) * 1024 + h * 512: (8 + dc) * 1024 + h * 512 + 512],
                               start=False, stop=False),
                      reads=[("nfm", t), ("w", "w2")], writes=[("ps_s", pb + h)])
                for rc in range(2):
                    A("T", lambda e, h=h, rc=rc, s_=s_, bank=bank:
                      e.matmul(bank[:, :], qfm[:, rc * 512 + s_ * 128: rc * 512 + (s_ + 1) * 128],
                               v2sb[:, rc * 1024 + h * 512: rc * 1024 + h * 512 + 512],
                               start=False, stop=(rc == 1)),
                      reads=[("qfm",), ("w", "v2")], writes=[("ps_s", pb + h)])
                A("V", lambda e, h=h, st=st, bank=bank:
                  e.tensor_tensor(out=outsb[st % 3][:, h * 512:(h + 1) * 512], in0=bank[:, :],
                                  in1=xb[st % 2][:, h * 512:(h + 1) * 512], op=AluOpType.add),
                  reads=[("ps_s", pb + h), ("xb", st % 2)], writes=[("outsb", st % 3)])
            A("G", lambda e, b=b, st=st:
              e.dma_start(out=out[b, st * 128:(st + 1) * 128, :], in_=outsb[st % 3][:, :]),
              reads=[("outsb", st % 3)], chan=("QST", st % 3), dma=True)
        if t + 2 < NT and PREP_IN_MAIN:
            emit_prep_compute(b, t + 2)

    def emit_weights():
        chunks = []
        u1v = U1.rearrange("(c p) r -> p c r", p=128)
        chunks.append((lambda: u1v[:, :, :], u1sb, 0, 2048))
        v1v = V1.rearrange("(c p) f -> p c f", p=128)
        for j in range(4):
            c, hh = j // 2, j % 2
            chunks.append((lambda c=c, hh=hh: v1v[:, c:c + 1, hh * 2048:(hh + 1) * 2048],
                           v1sb, c * 4096 + hh * 2048, 2048))
        u2v = U2.rearrange("(c p) r -> p c r", p=128)
        for j in range(4):
            chunks.append((lambda j=j: u2v[:, 8 * j:8 * (j + 1), :], u2sb, j * 2048, 2048))
        v2v = V2.rearrange("(c p) d -> p c d", p=128)
        chunks.append((lambda: v2v[:, :, :], v2sb, 0, 2048))
        w2v = W2.rearrange("(c p) d -> p c d", p=128)
        for j in range(8):
            chunks.append((lambda j=j: w2v[:, 2 * j:2 * (j + 1), :], w2sb, j * 2048, 2048))
        names = {id(u1sb): "u1", id(v1sb): "v1", id(u2sb): "u2", id(v2sb): "v2", id(w2sb): "w2"}
        eng_of = {"w2": "V", "u1": "S", "v1": "S", "u2": "S", "v2": "S"}
        for i, (apf, dest, off, n) in enumerate(chunks):
            stg = wstage[i % 2]
            A("SY", lambda e, apf=apf, stg=stg: e.dma_start(out=stg[:, :], in_=apf()),
              writes=[("wstage", i % 2)], chan=("QW", i % 2), dma=True)
            wname = names[id(dest)]
            if eng_of[wname] == "V":
                A("V", lambda e, stg=stg, dest=dest, off=off, n=n:
                  e.tensor_copy(dest[:, off:off + n], stg[:, :]),
                  reads=[("wstage", i % 2)], writes=[("w", wname)])
            else:
                A("S", lambda e, stg=stg, dest=dest, off=off, n=n:
                  e.copy(dest[:, off:off + n], stg[:, :]),
                  reads=[("wstage", i % 2)], writes=[("w", wname)])

    emit_stats(0, range(NT), nslot=7, sum_on_v=True)
    emit_colmath(0, 0, 16)
    for t in range(NT):
        emit_phase1_tile(0, t, nslot=6, cast_eng="S")
    emit_weights()
    for b in range(NB):
        nxt = b + 1
        for t in range(NT):
            emit_sim_tile(b, t)
            if nxt < NB:
                emit_stats(nxt, [t], nslot=6)
            if t == 1 and PREP_IN_MAIN:
                emit_prep_loads(b, 0)
        for t in range(2 if CF_IN_MAIN else NT):
            emit_cf_tile(b, t)
            if t < 2 and PREP_IN_MAIN:
                emit_prep_compute(b, t)
                if t == 0:
                    emit_prep_loads(b, 1)
        if not PREP_IN_MAIN:
            for t in range(2):
                emit_phase3_prep(b, t)
        if nxt < NB:
            emit_colmath(nxt, 0, 16)
        for t in range(NT):
            emit_phase3_main(b, t)
            if t + 2 < NT and not PREP_IN_MAIN:
                emit_phase3_prep(b, t + 2)
            if nxt < NB:
                emit_phase1_tile(nxt, t)

    sch.finalize()
    globals()["LAST_SCHED"] = sch
    with nc.Block() as block:
        @block.tensor
        def _(eng):
            sch.emit("T", eng, sems)

        @block.vector
        def _(eng):
            sch.emit("V", eng, sems)

        @block.scalar
        def _(eng):
            sch.emit("S", eng, sems)

        @block.gpsimd
        def _(eng):
            sch.emit("G", eng, sems)
            for k in range(3):
                n_store = sch.chan_count.get(("QST", k), 0)
                if n_store:
                    eng.wait_ge(sems[("QST", k)], 16 * n_store)

        @block.sync
        def _(eng):
            sch.emit("SY", eng, sems)
    es.close()
    return nc


def kernel(L0_attn_data, L0_attn_W1, L0_attn_b1, L0_attn_W2,
           L0_ffn_U1, L0_ffn_V1, L0_ffn_b1, L0_ffn_U2, L0_ffn_V2, L0_ffn_b2):
    n = 8
    def h(a):
        a = np.asarray(a)
        assert a.dtype == np.float32, a.dtype
        return a
    xfull = np.ascontiguousarray(h(L0_attn_data))
    com = {
        "W1": np.ascontiguousarray(h(L0_attn_W1).reshape(8, 128).T),
        "b1": np.ascontiguousarray(h(L0_attn_b1).reshape(8, 128).T),
        "W2": np.ascontiguousarray(h(L0_attn_W2)),
        "U1": np.ascontiguousarray(h(L0_ffn_U1)),
        "V1": np.ascontiguousarray(h(L0_ffn_V1)),
        "fb1": np.ascontiguousarray(h(L0_ffn_b1).reshape(32, 128).T),
        "U2": np.ascontiguousarray(h(L0_ffn_U2)),
        "V2": np.ascontiguousarray(h(L0_ffn_V2)),
        "b2": np.ascontiguousarray(h(L0_ffn_b2).reshape(1, 1024)),
        "idb": np.eye(128, dtype=np.float32).astype(ml_dtypes.bfloat16),
        "idf": np.eye(128, dtype=np.float32),
    }
    nc = build_program()
    in_maps = [dict(com, x=np.ascontiguousarray(xfull[2 * c:2 * c + 2])) for c in range(n)]
    res = run_bass_kernel_spmd(nc, in_maps, core_ids=list(range(n)))
    return np.concatenate([res.results[c]["out"] for c in range(n)], axis=0)


# version 36 (final)
# speedup vs baseline: 1.0239x; 1.0171x over previous
import numpy as np
import ml_dtypes
import concourse.bass as bass
import concourse.mybir as mybir
from concourse.bass_utils import run_bass_kernel_spmd
from concourse.alu_op_type import AluOpType

F32 = mybir.dt.float32
BF16 = mybir.dt.bfloat16
AFT = mybir.ActivationFunctionType
AX = mybir.AxisListType

CF_IN_MAIN = True
PREP_IN_MAIN = True
NB = 2
SEQ = 2048
D = 1024
DC = 8
NT = 4
NQ = 16

class _Op:
    __slots__ = ("stream", "chan", "fn", "dma", "pos", "chan_idx", "marked",
                 "waits", "drain", "clock", "tag")

class Sched:
    def __init__(self, streams, total_chans=()):
        self.streams = {s: [] for s in streams}
        self.total_chans = set(total_chans)
        self.chan_count = {}
        self.res = {}
        self.stream_clock = {s: {} for s in streams}
        self.last_drain = {s: -1 for s in streams}

    def add(self, stream, fn, reads=(), writes=(), chan=None, dma=False, tag=""):
        chan = chan or stream
        op = _Op()
        op.stream, op.chan, op.fn, op.dma, op.tag = stream, chan, fn, dma, tag
        op.pos = len(self.streams[stream])
        op.chan_idx = self.chan_count.get(chan, 0)
        self.chan_count[chan] = op.chan_idx + 1
        op.marked = dma
        op.waits = {}
        op.drain = False
        deps = []
        for k in reads:
            st = self.res.get(k)
            if st and st[0] is not None:
                deps.append((st[0], False))
        for k in writes:
            st = self.res.get(k)
            if st:
                if st[0] is not None:
                    deps.append((st[0], False))
                for r in st[1].values():
                    deps.append((r, True))
        clock = dict(self.stream_clock[stream])
        for dep, is_war in deps:
            if dep is op:
                continue
            if dep.dma:
                if dep.chan in self.total_chans:
                    idx = self.chan_count[dep.chan] - 1
                else:
                    idx = dep.chan_idx
                if clock.get(dep.chan, -1) < idx:
                    op.waits[dep.chan] = max(op.waits.get(dep.chan, -1), idx)
                    for c2, i2 in dep.clock.items():
                        if clock.get(c2, -1) < i2:
                            clock[c2] = i2
                    clock[dep.chan] = idx
            elif dep.stream == stream:
                if not is_war and stream != "T":
                    if clock.get(dep.chan, -1) < dep.chan_idx:
                        dep.marked = True
                        op.waits[dep.chan] = max(op.waits.get(dep.chan, -1), dep.chan_idx)
                        clock[dep.chan] = dep.chan_idx
            else:
                if clock.get(dep.chan, -1) < dep.chan_idx:
                    dep.marked = True
                    op.waits[dep.chan] = max(op.waits.get(dep.chan, -1), dep.chan_idx)
                    for c2, i2 in dep.clock.items():
                        if clock.get(c2, -1) < i2:
                            clock[c2] = i2
                    clock[dep.chan] = dep.chan_idx
        op.clock = clock
        self.stream_clock[stream] = clock
        for k in reads:
            self.res.setdefault(k, [None, {}])[1][stream] = op
        for k in writes:
            self.res[k] = [op, {}]
        self.streams[stream].append(op)
        return op

    def finalize(self):
        self.mark_val = {}
        per_chan_ops = {}
        for s, ops in self.streams.items():
            for o in ops:
                per_chan_ops.setdefault(o.chan, []).append(o)
        for c, ops in per_chan_ops.items():
            ops.sort(key=lambda o: o.chan_idx)
            acc = 0
            vals = []
            for o in ops:
                if o.marked:
                    acc += 16 if o.dma else 1
                vals.append(acc)
            self.mark_val[c] = vals

    def wait_value(self, chan, idx):
        return self.mark_val[chan][idx]

    def emit(self, stream, eng, sems):
        for o in self.streams[stream]:
            if o.drain:
                eng.drain()
            for c, idx in sorted(o.waits.items(), key=lambda kv: str(kv[0])):
                v = self.wait_value(c, idx)
                if v > 0:
                    eng.wait_ge(sems[c], v)
            inst = o.fn(eng)
            if o.marked:
                if inst is None:
                    inst = eng.nop()
                inst.then_inc(sems[o.chan], 16 if o.dma else 1)


def build_program():
    nc = bass.Bass("TRN2", target_bir_lowering=False)
    from contextlib import ExitStack
    es = ExitStack()

    x = nc.dram_tensor("x", [NB, SEQ, D], F32, kind="ExternalInput")
    W1 = nc.dram_tensor("W1", [128, 8], F32, kind="ExternalInput")
    b1 = nc.dram_tensor("b1", [128, 8], F32, kind="ExternalInput")
    W2 = nc.dram_tensor("W2", [2 * D, D], F32, kind="ExternalInput")
    U1 = nc.dram_tensor("U1", [D, 256], F32, kind="ExternalInput")
    V1 = nc.dram_tensor("V1", [256, 4096], F32, kind="ExternalInput")
    fb1 = nc.dram_tensor("fb1", [128, 32], F32, kind="ExternalInput")
    U2 = nc.dram_tensor("U2", [4096, 256], F32, kind="ExternalInput")
    V2 = nc.dram_tensor("V2", [256, D], F32, kind="ExternalInput")
    b2 = nc.dram_tensor("b2", [1, D], F32, kind="ExternalInput")
    idb = nc.dram_tensor("idb", [128, 128], BF16, kind="ExternalInput")
    idf = nc.dram_tensor("idf", [128, 128], F32, kind="ExternalInput")
    out = nc.dram_tensor("out", [NB, SEQ, D], F32, kind="ExternalOutput")

    def sb(name, shape, dt=F32):
        return es.enter_context(nc.sbuf_tensor(name, shape, dt))

    def ps(name, shape, dt=F32):
        return es.enter_context(nc.psum_tensor(name, shape, dt))

    w2sb = sb("w2sb", [128, 16 * 1024], BF16)
    u1sb = sb("u1sb", [128, 8 * 256], BF16)
    v1sb = sb("v1sb", [128, 2 * 4096], BF16)
    u2sb = sb("u2sb", [128, 32 * 256], BF16)
    v2sb = sb("v2sb", [128, 2 * 1024], BF16)
    wstage = [sb(f"wstage{i}", [128, 2048]) for i in range(2)]
    w1c = sb("w1c", [128, 8])
    b1c = sb("b1c", [128, 8])
    fb1c = sb("fb1c", [128, 32])
    cscr = sb("cscr", [1, 512])
    b2b = sb("b2b", [128, 1024])
    onesf = sb("onesf", [1, 128])
    onesel = sb("onesel", [128, 16], BF16)
    identbf = sb("identbf", [128, 128], BF16)
    identf = sb("identf", [128, 128])

    nfm = sb("nfm", [128, 8 * 2048], BF16)
    xa = [sb(f"xa{i}", [128, 1024]) for i in range(2)]
    xr = [sb(f"xr{i}", [128, 1024]) for i in range(2)]
    xb = [wstage[0][:, 0:1024], wstage[0][:, 1024:2048]]
    outsb = [sb(f"outsb{i}", [128, 1024]) for i in range(3)]
    xnfm = [sb(f"xnfm{i}", [128, 8 * 512], BF16) for i in range(2)]
    ntok = [sb(f"ntok{i}", [128, 1024], BF16) for i in range(2)]
    xntok = [sb(f"xntok{i}", [128, 1024], BF16) for i in range(4)]
    pfm = [sb(f"pfm{i}", [128, 2 * 512], BF16) for i in range(2)]
    qfm = sb("qfm", [128, 2 * 512], BF16)
    hbf = [sb(f"hbf{i}", [128, 512], BF16) for i in range(2)]
    e512 = [wstage[1][:, 0:512], wstage[1][:, 512:1024]]
    vjunk = sb("vjunk", [128, 2048], BF16)
    cpart = [sb(f"cpart{i}", [128, 4]) for i in range(2)]
    call = sb("call", [128, 16])
    crow4 = [sb(f"crow4{i}", [4, 128]) for i in range(2)]
    crow1 = sb("crow1", [1, 2048])
    cols = {}
    for n in ["sum", "ssq", "mean", "var", "u", "ru", "sq", "rs", "rs2",
              "scaleA", "biasA", "invs", "biasB", "t1", "t2"]:
        cols[n] = [sb(f"c_{n}{p}", [128, 16]) for p in range(2)]
    sq32 = [sb(f"sq32_{p}", [128, 32]) for p in range(2)]
    iv32 = [sb(f"iv32_{p}", [128, 32]) for p in range(2)]

    s0 = ps("s0", [128, 512]); s1 = ps("s1", [128, 512])
    s2 = ps("s2", [128, 512]); s3 = ps("s3", [128, 512])
    q0 = ps("q0", [128, 512]); q1 = ps("q1", [128, 512])
    tb = [ps("tb0", [128, 1024], BF16), ps("tb1", [128, 1024], BF16)]
    sbank = [s0, s1, s2, s3]

    streams = ["T", "V", "S", "G", "SY"]
    dma_chans = (["QC"] + [("QW", i) for i in range(2)] + [("QXA", i) for i in range(2)]
                 + [("QXR", i) for i in range(2)] + [("QXO", i) for i in range(2)]
                 + [("QXO", 2)] + [("QTN", i) for i in range(2)]
                 + [("QST", i) for i in range(3)])
    sch = Sched(streams, total_chans=["QC"])
    sems = {}
    for c in streams + dma_chans:
        nm = c if isinstance(c, str) else c[0] + str(c[1])
        sems[c] = es.enter_context(nc.semaphore("sem_" + nm))

    A = sch.add

    K_nfm = [("nfm", t) for t in range(NT)]
    K_crow1 = [("crow1", t) for t in range(NT)]
    K_w = "wALL"
    A("SY", lambda e: e.dma_start(out=identbf[:, :], in_=idb[:, :]),
      writes=[("identbf",)], chan="QC", dma=True)
    A("SY", lambda e: e.dma_start(out=identf[:, :], in_=idf[:, :]),
      writes=[("identf",)], chan="QC", dma=True)
    A("SY", lambda e: e.dma_start(out=w1c[:, :], in_=W1[:, :]),
      writes=[("w1c",)], chan="QC", dma=True)
    A("SY", lambda e: e.dma_start(out=b1c[:, :], in_=b1[:, :]),
      writes=[("b1c",)], chan="QC", dma=True)
    A("SY", lambda e: e.dma_start(out=fb1c[:, :], in_=fb1[:, :]),
      writes=[("fb1c",)], chan="QC", dma=True)
    b2bc, _ = bass.broadcast_tensor_aps(b2[0:1, :], b2b[:, :])
    A("SY", lambda e: e.dma_start(out=b2b[:, :], in_=b2bc),
      writes=[("b2b",)], chan="QC", dma=True)
    A("V", lambda e: e.memset(onesf[:, :], 1.0), writes=[("onesf",)])
    A("V", lambda e: e.memset(onesel[:, :], 0.0), writes=[("onesel",)])
    for kc in range(4):
        A("V", lambda e, kc=kc: e.memset(onesel[:, 4 * kc + kc:4 * kc + kc + 1], 1.0),
          reads=[("onesel",)], writes=[("onesel",)])

    def xslot(st, nslot):
        i = st % nslot
        if i < 2:
            return xa[i], ("xa", i), ("QXA", i)
        if i < 4:
            return xr[i - 2], ("xr", i - 2), ("QXR", i - 2)
        return outsb[i - 4], ("outsb", i - 4), ("QXO", i - 4)

    def emit_stats(b, tiles, nslot=4, fused=False, sum_on_v=False):
        p = b % 2
        for t in tiles:
            for s_ in range(4):
                st = 4 * t + s_
                buf, bkey, bchan = xslot(st, nslot)
                A("SY", lambda e, b=b, st=st, buf=buf:
                  e.dma_start(out=buf[:, :], in_=x[b, st * 128:(st + 1) * 128, :]),
                  reads=[], writes=[bkey], chan=bchan, dma=True)
                if sum_on_v:
                    A("V", lambda e, buf=buf, p=p, st=st:
                      e.tensor_scalar(out=vjunk[:, 0:1024], in0=buf[:, :], scalar1=0.0,
                                      scalar2=None, op0=AluOpType.add, op1=AluOpType.add,
                                      accum_out=cols["sum"][p][:, st:st + 1]),
                      reads=[bkey], writes=[("c_sum", p, st), ("vj", 0), ("vj", 1)])
                else:
                    A("S", lambda e, buf=buf, p=p, st=st:
                      e.activation(out=outsb[2][:, :], in_=buf[:, :], func=AFT.Identity,
                                   accum_out=cols["sum"][p][:, st:st + 1]),
                      reads=[bkey], writes=[("c_sum", p, st), ("outsb", 2)])
                if b == 0:
                    A("S", lambda e, buf=buf, p=p, st=st:
                      e.activation(out=vjunk[:, 1024:2048], in_=buf[:, :], func=AFT.Square,
                                   accum_out=cols["ssq"][p][:, st:st + 1]),
                      reads=[bkey], writes=[("c_ssq", p, st), ("vj", 2), ("vj", 3)])
                else:
                    A("S", lambda e, buf=buf, p=p, st=st:
                      e.activation(out=outsb[2][:, :], in_=buf[:, :], func=AFT.Square,
                                   accum_out=cols["ssq"][p][:, st:st + 1]),
                      reads=[bkey], writes=[("c_ssq", p, st), ("outsb", 2)])
            if fused:
                emit_colmath(b, 4 * t, 4 * t + 4)
                C = {n: cols[n][p] for n in cols}
                for s_ in range(4):
                    st = 4 * t + s_
                    buf, bkey, bchan = xslot(st, nslot)
                    nt_ = ntok[st % 2]
                    A("V", lambda e, buf=buf, nt_=nt_, st=st, C=C:
                      e.tensor_scalar(out=nt_[:, :], in0=buf[:, :],
                                      scalar1=C["scaleA"][:, st:st + 1],
                                      scalar2=C["biasA"][:, st:st + 1],
                                      op0=AluOpType.mult, op1=AluOpType.add),
                      reads=[bkey, ("cmA", p, st)], writes=[("ntok", st % 2)])
                    bank = tb[st % 2]
                    for dc in range(DC):
                        A("T", lambda e, nt_=nt_, bank=bank, dc=dc:
                          e.transpose(bank[:, dc * 128:(dc + 1) * 128],
                                      nt_[:, dc * 128:(dc + 1) * 128], identbf[:, :]),
                          reads=[("ntok", st % 2), ("identbf",)], writes=[("ps_tb", st % 2)])
                    A("V", lambda e, bank=bank, t=t, s_=s_:
                      e.tensor_copy(nfm.rearrange("p (c f) -> p c f", c=8)[:, :, t * 512 + s_ * 128: t * 512 + (s_ + 1) * 128],
                                    bank[:, :].rearrange("p (c f) -> p c f", c=8)),
                      reads=[("ps_tb", st % 2)], writes=[("nfm", t)])

    def emit_colmath(b, s0, s1):
        p = b % 2
        C = {n: cols[n][p][:, s0:s1] for n in cols}
        K = lambda n: ("cm", p, n, s0)
        rd_sum = [("c_sum", p, st) for st in range(s0, s1)]
        rd_ssq = [("c_ssq", p, st) for st in range(s0, s1)]
        def vop(fn, reads, writes):
            A("V", fn, reads=reads, writes=writes)
        vop(lambda e: e.tensor_scalar(out=C["mean"], in0=C["sum"],
            scalar1=1.0 / D, scalar2=None, op0=AluOpType.mult), rd_sum, [K("mean")])
        vop(lambda e: e.tensor_scalar(out=C["t1"], in0=C["ssq"],
            scalar1=1.0 / D, scalar2=None, op0=AluOpType.mult), rd_ssq, [K("t1")])
        vop(lambda e: e.tensor_tensor(out=C["t2"], in0=C["mean"],
            in1=C["mean"], op=AluOpType.mult), [K("mean")], [K("t2")])
        vop(lambda e: e.tensor_tensor(out=C["var"], in0=C["t1"],
            in1=C["t2"], op=AluOpType.subtract), [K("t1"), K("t2")], [K("var")])
        vop(lambda e: e.tensor_scalar(out=C["u"], in0=C["var"],
            scalar1=1e-5, scalar2=None, op0=AluOpType.add), [K("var")], [K("u")])
        vop(lambda e: e.reciprocal(C["ru"], C["u"]), [K("u")], [K("ru")])
        vop(lambda e: e.tensor_tensor(out=C["t1"], in0=C["var"],
            in1=C["ru"], op=AluOpType.mult), [K("var"), K("ru")], [K("t1")])
        vop(lambda e: e.tensor_scalar(out=C["sq"], in0=C["t1"],
            scalar1=float(D), scalar2=1e-12, op0=AluOpType.mult, op1=AluOpType.max),
            [K("t1")], [K("sq")])
        vop(lambda e: e.reciprocal(C["rs"], C["sq"]), [K("sq")], [K("rs")])
        sqa = sq32[p][:, s0:s1]; sqb = sq32[p][:, 16 + s0:16 + s1]
        iva = iv32[p][:, s0:s1]; ivb = iv32[p][:, 16 + s0:16 + s1]
        vop(lambda e: e.tensor_copy(sqa, C["ru"]), [K("ru")], [K("s32a")])
        vop(lambda e: e.tensor_copy(sqb, C["rs"]), [K("rs")], [K("s32b")])
        A("S", lambda e: e.activation(out=iva, in_=sqa, func=AFT.Sqrt),
          reads=[K("s32a")], writes=[K("iva")])
        A("S", lambda e: e.activation(out=ivb, in_=sqb, func=AFT.Sqrt),
          reads=[K("s32b")], writes=[K("ivb")])
        def newton(ycols, ykey, acol, akey, t1, outc, outk):
            vop(lambda e: e.tensor_tensor(out=t1, in0=ycols, in1=ycols,
                op=AluOpType.mult), [ykey], [K(outk + "t1")])
            vop(lambda e: e.tensor_tensor(out=t1, in0=t1, in1=acol,
                op=AluOpType.mult), [K(outk + "t1"), akey], [K(outk + "t1")])
            vop(lambda e: e.tensor_scalar(out=t1, in0=t1, scalar1=-0.5,
                scalar2=1.5, op0=AluOpType.mult, op1=AluOpType.add),
                [K(outk + "t1")], [K(outk + "t1")])
            vop(lambda e: e.tensor_tensor(out=outc, in0=ycols, in1=t1,
                op=AluOpType.mult), [ykey, K(outk + "t1")], [K(outk)])
        newton(iva, K("iva"), C["u"], K("u"), C["t1"], C["invs"], "invs")
        newton(ivb, K("ivb"), C["sq"], K("sq"), C["t2"], C["rs2"], "rs2")
        vop(lambda e: e.tensor_tensor(out=C["scaleA"], in0=C["invs"],
            in1=C["rs2"], op=AluOpType.mult),
            [K("invs"), K("rs2")], [K("scaleA")])
        vop(lambda e: e.scalar_tensor_tensor(out=C["biasA"], in0=C["mean"],
            scalar=-1.0, in1=C["scaleA"], op0=AluOpType.mult, op1=AluOpType.mult),
            [K("mean"), K("scaleA")], [K("biasA")] + [("cmA", p, st) for st in range(s0, s1)])
        vop(lambda e: e.scalar_tensor_tensor(out=C["biasB"], in0=C["mean"],
            scalar=-1.0, in1=C["invs"], op0=AluOpType.mult, op1=AluOpType.mult),
            [K("mean"), K("invs")], [K("biasB")] + [("cmB", p, st) for st in range(s0, s1)])

    def emit_phase1_tile(b, t, nslot=2, cast_eng="V"):
        p = b % 2
        C = {n: cols[n][p] for n in cols}
        for s_ in range(4):
            st = 4 * t + s_
            buf, bkey, bchan = xslot(st, nslot)
            A("SY", lambda e, b=b, st=st, buf=buf:
              e.dma_start(out=buf[:, :], in_=x[b, st * 128:(st + 1) * 128, :]),
              writes=[bkey], chan=bchan, dma=True)
            nt_ = ntok[st % 2]
            if cast_eng == "V":
                A("V", lambda e, buf=buf, nt_=nt_, st=st:
                  e.tensor_scalar(out=nt_[:, :], in0=buf[:, :],
                                  scalar1=C["scaleA"][:, st:st + 1], scalar2=C["biasA"][:, st:st + 1],
                                  op0=AluOpType.mult, op1=AluOpType.add),
                  reads=[bkey, ("cmA", p, st)],
                  writes=[("ntok", st % 2)])
            else:
                A("S", lambda e, buf=buf, nt_=nt_, st=st:
                  e.activation(out=nt_[:, :], in_=buf[:, :], func=AFT.Identity,
                               scale=C["scaleA"][:, st:st + 1], bias=C["biasA"][:, st:st + 1]),
                  reads=[bkey, ("cmA", p, st)],
                  writes=[("ntok", st % 2)])
            bank = tb[st % 2]
            for dc in range(DC):
                A("T", lambda e, nt_=nt_, bank=bank, dc=dc:
                  e.transpose(bank[:, dc * 128:(dc + 1) * 128], nt_[:, dc * 128:(dc + 1) * 128],
                              identbf[:, :]),
                  reads=[("ntok", st % 2), ("identbf",)], writes=[("ps_tb", st % 2)])
            if cast_eng == "S":
                A("V", lambda e, bank=bank, t=t, s_=s_:
                  e.tensor_copy(nfm.rearrange("p (c f) -> p c f", c=8)[:, :, t * 512 + s_ * 128: t * 512 + (s_ + 1) * 128],
                                bank[:, :].rearrange("p (c f) -> p c f", c=8)),
                  reads=[("ps_tb", st % 2)], writes=[("nfm", t)], tag=f"nfmev{b}_{t}{s_}")
            else:
                A("S", lambda e, bank=bank, t=t, s_=s_:
                  e.copy(nfm.rearrange("p (c f) -> p c f", c=8)[:, :, t * 512 + s_ * 128: t * 512 + (s_ + 1) * 128],
                         bank[:, :].rearrange("p (c f) -> p c f", c=8)),
                  reads=[("ps_tb", st % 2)], writes=[("nfm", t)], tag=f"nfmev{b}_{t}{s_}")

    sim_state = {}

    def emit_sim_tile(b, t):
        if t == 0:
            sim_state["bc"] = 0
            sim_state["cs"] = 0
            sim_state["pending"] = []
            sim_state["ncs"] = sum(3 - i // 4 for i in range(16))
        def flush_pending():
            for (vslot, kc) in sim_state["pending"]:
                ci = sim_state["cs"]; sim_state["cs"] += 1
                A("T", lambda e, vslot=vslot, kc=kc, ci=ci:
                  e.matmul(q1[0:4, 0:512], onesel[:, 4 * kc:4 * kc + 4],
                           vjunk[:, vslot * 512:(vslot + 1) * 512],
                           start=(ci == 0), stop=(ci == sim_state["ncs"] - 1)),
                  reads=[("vj", vslot), ("onesel",)], writes=[("ps_q1",)], tag=f"cs{b}")
            sim_state["pending"] = []
        for qq in range(4):
            q = 4 * t + qq
            cp = cpart[q % 2]
            nb = 4 - q // 4
            for j, kc in enumerate(range(q // 4, 4)):
                bank = sbank[sim_state["bc"] % 4]
                bkkey = ("ps_s", sim_state["bc"] % 4)
                vslot = sim_state["bc"] % 4
                for dc in range(DC):
                    A("T", lambda e, bank=bank, dc=dc, q=q, kc=kc:
                      e.matmul(bank[:, :],
                               nfm[:, dc * 2048 + q * 128: dc * 2048 + (q + 1) * 128],
                               nfm[:, dc * 2048 + kc * 512: dc * 2048 + (kc + 1) * 512],
                               start=(dc == 0), stop=(dc == DC - 1)),
                      reads=[("nfm", tt) for tt in range(NT)],
                      writes=[bkkey], tag=f"sim{b}_{q}_{kc}")
                flush_pending()
                A("V", lambda e, bank=bank, cp=cp, j=j, vslot=vslot:
                  e.tensor_scalar(out=vjunk[:, vslot * 512:(vslot + 1) * 512],
                                  in0=bank[:, :], scalar1=0.0,
                                  scalar2=None, op0=AluOpType.max, op1=AluOpType.add,
                                  accum_out=cp[:, j:j + 1]),
                  reads=[bkkey], writes=[("cpart", q % 2, j), ("vj", vslot)])
                if kc > q // 4:
                    sim_state["pending"].append((vslot, kc))
                sim_state["bc"] += 1
            A("V", lambda e, cp=cp, q=q, nb=nb:
              e.reduce_sum(call[:, q:q + 1], cp[:, 0:nb], axis=AX.X),
              reads=[("cpart", q % 2, j) for j in range(nb)], writes=[("call", t)])
        c4 = crow4[t % 2]
        A("T", lambda e, t=t: e.transpose(q0[0:4, 0:128], call[:, 4 * t:4 * t + 4], identf[:, :]),
          reads=[("call", t), ("identf",)], writes=[("ps_q0",)])
        A("V", lambda e, c4=c4: e.tensor_copy(c4[:, :], q0[0:4, 0:128]),
          reads=[("ps_q0",)], writes=[("crow4", t % 2)])
        for qi in range(4):
            A("SY", lambda e, c4=c4, t=t, qi=qi:
              e.dma_start(out=crow1[0:1, t * 512 + qi * 128: t * 512 + (qi + 1) * 128],
                          in_=c4[qi:qi + 1, :]),
              reads=[("crow4", t % 2)], writes=[("crow1", t)], chan=("QTN", t % 2), dma=True)
        if t == NT - 1:
            flush_pending()
            A("V", lambda e: e.tensor_copy(wstage[1][0:4, 1024:1536], q1[0:4, 0:512]),
              reads=[("ps_q1",)], writes=[("colsb",), ("wstage", 1)])
            for kc in range(1, 4):
                A("SY", lambda e, kc=kc:
                  e.dma_start(out=cscr[0:1, 0:512], in_=wstage[1][kc:kc + 1, 1024:1536]),
                  reads=[("colsb",)], writes=[("cscr",)], chan=("QTN", 0), dma=True)
                A("V", lambda e, kc=kc:
                  e.tensor_tensor(out=crow1[0:1, kc * 512:(kc + 1) * 512],
                                  in0=crow1[0:1, kc * 512:(kc + 1) * 512],
                                  in1=cscr[0:1, 0:512], op=AluOpType.add),
                  reads=[("cscr",), ("crow1", kc)], writes=[("crow1", kc)])

    def emit_cf_tile(b, t):
        qb = [q0, q1][t % 2]
        qk = ("ps_q0",) if t % 2 == 0 else ("ps_q1",)
        A("T", lambda e, t=t, qb=qb: e.matmul(qb[:, :], onesf[:, :],
                                       crow1[0:1, t * 512:(t + 1) * 512], start=True, stop=True),
          reads=[("onesf",), ("crow1", t)], writes=[qk], tag=f"T2_{b}_{t}")
        for dc in range(DC):
            ebuf = e512[dc % 2]
            A("S", lambda e, ebuf=ebuf, dc=dc, qb=qb:
              e.activation(out=ebuf[:, :], in_=qb[:, :], func=AFT.Exp,
                           scale=w1c[:, dc:dc + 1], bias=b1c[:, dc:dc + 1]),
              reads=[qk, ("w1c",), ("b1c",)], writes=[("e512", dc % 2), ("wstage", 1)])
            A("S", lambda e, ebuf=ebuf, dc=dc, t=t:
              e.activation(out=nfm[:, dc * 2048 + t * 512: dc * 2048 + (t + 1) * 512],
                           in_=ebuf[:, :], func=AFT.Ln, bias=1.0, scale=1.0),
              reads=[("e512", dc % 2)], writes=[("nfm", t)], tag=f"cf{b}_{t}_{dc}")

    def emit_prep_xload(b, t):
        for s_ in range(4):
            st = 4 * t + s_
            bufr = xr[st % 2]
            A("SY", lambda e, b=b, st=st, bufr=bufr:
              e.dma_start(out=bufr[:, :], in_=x[b, st * 128:(st + 1) * 128, :]),
              writes=[("xr", st % 2)], chan=("QXR", st % 2), dma=True)
            emit_prep_cast(b, st)

    def emit_prep_cast(b, st):
        p = b % 2
        C = {n: cols[n][p] for n in cols}
        bufr = xr[st % 2]
        xt_ = xntok[st % 4]
        A("V", lambda e, bufr=bufr, xt_=xt_, st=st:
          e.tensor_scalar(out=xt_[:, :], in0=bufr[:, :],
                          scalar1=C["invs"][:, st:st + 1], scalar2=C["biasB"][:, st:st + 1],
                          op0=AluOpType.mult, op1=AluOpType.add),
          reads=[("xr", st % 2), ("cmB", p, st)],
          writes=[("xntok", st % 4)])

    def emit_prep_loads(b, t):
        emit_prep_xload(b, t)

    def emit_prep_compute(b, t):
        xf = xnfm[t % 2]
        for s_ in range(4):
            st = 4 * t + s_
            xt_ = xntok[st % 4]
            bank = tb[st % 2]
            for dc in range(DC):
                A("T", lambda e, xt_=xt_, bank=bank, dc=dc:
                  e.transpose(bank[:, dc * 128:(dc + 1) * 128], xt_[:, dc * 128:(dc + 1) * 128],
                              identbf[:, :]),
                  reads=[("xntok", st % 4), ("identbf",)], writes=[("ps_tb", st % 2)])
            A("V", lambda e, bank=bank, s_=s_, xf=xf:
              e.tensor_copy(xf.rearrange("p (c f) -> p c f", c=8)[:, :, s_ * 128:(s_ + 1) * 128],
                            bank[:, :].rearrange("p (c f) -> p c f", c=8)),
              reads=[("ps_tb", st % 2)], writes=[("xnfm", t % 2)])
        pf = pfm[t % 2]
        for rc in range(2):
            for dc in range(DC):
                A("T", lambda e, rc=rc, dc=dc, xf=xf:
                  e.matmul(sbank[rc][:, :], u1sb[:, dc * 256 + rc * 128: dc * 256 + (rc + 1) * 128],
                           xf[:, dc * 512:(dc + 1) * 512], start=(dc == 0), stop=(dc == DC - 1)),
                  reads=[("xnfm", t % 2), ("w", "u1")], writes=[("ps_s", rc)], tag=f"P{b}_{t}")
            A("V", lambda e, rc=rc, pf=pf: e.tensor_copy(pf[:, rc * 512:(rc + 1) * 512], sbank[rc][:, :]),
              reads=[("ps_s", rc)], writes=[("pfm", t % 2)])

    def emit_phase3_prep(b, t):
        emit_prep_loads(b, t)
        emit_prep_compute(b, t)

    def emit_phase3_main(b, t):
        p = b % 2
        xf = xnfm[t % 2]
        pf = pfm[t % 2]
        if t + 2 < NT and PREP_IN_MAIN:
            emit_prep_loads(b, t + 2)
        def emit_H(fc):
            hb = sbank[2 + (fc % 2)]
            A("T", lambda e, fc=fc, hb=hb:
              e.matmul(hb[:, :], v1sb[:, 0 * 4096 + fc * 128:0 * 4096 + (fc + 1) * 128],
                       pf[:, 0:512], start=True, stop=False),
              reads=[("pfm", t % 2), ("w", "v1")], writes=[("ps_s", 2 + (fc % 2))])
            A("T", lambda e, fc=fc, hb=hb:
              e.matmul(hb[:, :], v1sb[:, 1 * 4096 + fc * 128:1 * 4096 + (fc + 1) * 128],
                       pf[:, 512:1024], start=False, stop=True),
              reads=[("pfm", t % 2), ("w", "v1")], writes=[("ps_s", 2 + (fc % 2))])
            A("S", lambda e, fc=fc, hb=hb:
              e.activation(out=hbf[fc % 2][:, :], in_=hb[:, :], func=AFT.Gelu_apprx_tanh,
                           scale=1.0, bias=fb1c[:, fc:fc + 1]),
              reads=[("ps_s", 2 + (fc % 2)), ("fb1c",)], writes=[("hbf", fc % 2)])
        def emit_Q(fc):
            for rc in range(2):
                A("T", lambda e, fc=fc, rc=rc:
                  e.matmul([q0, q1][rc][:, :],
                           u2sb[:, fc * 256 + rc * 128: fc * 256 + (rc + 1) * 128],
                           hbf[fc % 2][:, :], start=(fc == 0), stop=(fc == 31)),
                  reads=[("hbf", fc % 2), ("w", "u2")],
                  writes=[("ps_q0",) if rc == 0 else ("ps_q1",)], tag=f"Q{b}_{t}")
        emit_H(0)
        for fc in range(1, 32):
            emit_H(fc)
            emit_Q(fc - 1)
        emit_Q(31)
        for rc in range(2):
            A("V", lambda e, rc=rc: e.tensor_copy(qfm[:, rc * 512:(rc + 1) * 512], [q0, q1][rc][:, :]),
              reads=[("ps_q0",) if rc == 0 else ("ps_q1",)], writes=[("qfm",)])
        if t + 2 < NT and CF_IN_MAIN:
            emit_cf_tile(b, t + 2)
        for s_ in range(4):
            st = 4 * t + s_
            pb = 2 * (s_ % 2)
            bufr = xr[st % 2]
            A("SY", lambda e, b=b, st=st, bufr=bufr:
              e.dma_start(out=bufr[:, :], in_=x[b, st * 128:(st + 1) * 128, :]),
              writes=[("xr", st % 2)], chan=("QXR", st % 2), dma=True)
            A("G", lambda e, bufr=bufr, st=st:
              e.tensor_tensor(out=xb[st % 2][:, :], in0=bufr[:, :], in1=b2b[:, :],
                              op=AluOpType.add),
              reads=[("xr", st % 2), ("b2b",)], writes=[("xb", st % 2), ("wstage", 0)])
            for h in range(2):
                bank = sbank[pb + h]
                for dc in range(DC):
                    A("T", lambda e, h=h, dc=dc, s_=s_, bank=bank, xf=xf:
                      e.matmul(bank[:, :], xf[:, dc * 512 + s_ * 128: dc * 512 + (s_ + 1) * 128],
                               w2sb[:, dc * 1024 + h * 512: dc * 1024 + h * 512 + 512],
                               start=(dc == 0), stop=False),
                      reads=[("xnfm", t % 2), ("w", "w2")], writes=[("ps_s", pb + h)], tag=f"F{b}_{t}_{s_}")
                for dc in range(DC):
                    A("T", lambda e, h=h, dc=dc, s_=s_, t=t, bank=bank:
                      e.matmul(bank[:, :],
                               nfm[:, dc * 2048 + t * 512 + s_ * 128: dc * 2048 + t * 512 + (s_ + 1) * 128],
                               w2sb[:, (8 + dc) * 1024 + h * 512: (8 + dc) * 1024 + h * 512 + 512],
                               start=False, stop=False),
                      reads=[("nfm", t), ("w", "w2")], writes=[("ps_s", pb + h)])
                for rc in range(2):
                    A("T", lambda e, h=h, rc=rc, s_=s_, bank=bank:
                      e.matmul(bank[:, :], qfm[:, rc * 512 + s_ * 128: rc * 512 + (s_ + 1) * 128],
                               v2sb[:, rc * 1024 + h * 512: rc * 1024 + h * 512 + 512],
                               start=False, stop=(rc == 1)),
                      reads=[("qfm",), ("w", "v2")], writes=[("ps_s", pb + h)])
                A("V", lambda e, h=h, st=st, bank=bank:
                  e.tensor_tensor(out=outsb[st % 3][:, h * 512:(h + 1) * 512], in0=bank[:, :],
                                  in1=xb[st % 2][:, h * 512:(h + 1) * 512], op=AluOpType.add),
                  reads=[("ps_s", pb + h), ("xb", st % 2)], writes=[("outsb", st % 3)])
            A("G", lambda e, b=b, st=st:
              e.dma_start(out=out[b, st * 128:(st + 1) * 128, :], in_=outsb[st % 3][:, :]),
              reads=[("outsb", st % 3)], chan=("QST", st % 3), dma=True)
        if t + 2 < NT and PREP_IN_MAIN:
            emit_prep_compute(b, t + 2)

    def emit_weights():
        chunks = []
        u1v = U1.rearrange("(c p) r -> p c r", p=128)
        chunks.append((lambda: u1v[:, :, :], u1sb, 0, 2048))
        v1v = V1.rearrange("(c p) f -> p c f", p=128)
        for j in range(4):
            c, hh = j // 2, j % 2
            chunks.append((lambda c=c, hh=hh: v1v[:, c:c + 1, hh * 2048:(hh + 1) * 2048],
                           v1sb, c * 4096 + hh * 2048, 2048))
        u2v = U2.rearrange("(c p) r -> p c r", p=128)
        for j in range(4):
            chunks.append((lambda j=j: u2v[:, 8 * j:8 * (j + 1), :], u2sb, j * 2048, 2048))
        v2v = V2.rearrange("(c p) d -> p c d", p=128)
        chunks.append((lambda: v2v[:, :, :], v2sb, 0, 2048))
        w2v = W2.rearrange("(c p) d -> p c d", p=128)
        for j in range(8):
            chunks.append((lambda j=j: w2v[:, 2 * j:2 * (j + 1), :], w2sb, j * 2048, 2048))
        names = {id(u1sb): "u1", id(v1sb): "v1", id(u2sb): "u2", id(v2sb): "v2", id(w2sb): "w2"}
        eng_of = {"w2": "V", "u1": "S", "v1": "S", "u2": "S", "v2": "S"}
        for i, (apf, dest, off, n) in enumerate(chunks):
            stg = wstage[i % 2]
            A("SY", lambda e, apf=apf, stg=stg: e.dma_start(out=stg[:, :], in_=apf()),
              writes=[("wstage", i % 2)], chan=("QW", i % 2), dma=True)
            wname = names[id(dest)]
            if eng_of[wname] == "V":
                A("V", lambda e, stg=stg, dest=dest, off=off, n=n:
                  e.tensor_copy(dest[:, off:off + n], stg[:, :]),
                  reads=[("wstage", i % 2)], writes=[("w", wname)])
            else:
                A("S", lambda e, stg=stg, dest=dest, off=off, n=n:
                  e.copy(dest[:, off:off + n], stg[:, :]),
                  reads=[("wstage", i % 2)], writes=[("w", wname)])

    emit_stats(0, range(NT), nslot=7, sum_on_v=True)
    emit_colmath(0, 0, 16)
    for t in range(NT):
        emit_phase1_tile(0, t, nslot=7, cast_eng="S")
    emit_weights()
    for b in range(NB):
        nxt = b + 1
        for t in range(NT):
            emit_sim_tile(b, t)
            if nxt < NB:
                emit_stats(nxt, [t], nslot=6)
            if t == 1 and PREP_IN_MAIN:
                emit_prep_loads(b, 0)
        for t in range(2 if CF_IN_MAIN else NT):
            emit_cf_tile(b, t)
            if t < 2 and PREP_IN_MAIN:
                emit_prep_compute(b, t)
                if t == 0:
                    emit_prep_loads(b, 1)
        if not PREP_IN_MAIN:
            for t in range(2):
                emit_phase3_prep(b, t)
        if nxt < NB:
            emit_colmath(nxt, 0, 16)
        for t in range(NT):
            emit_phase3_main(b, t)
            if t + 2 < NT and not PREP_IN_MAIN:
                emit_phase3_prep(b, t + 2)
            if nxt < NB:
                emit_phase1_tile(nxt, t)

    sch.finalize()
    globals()["LAST_SCHED"] = sch
    with nc.Block() as block:
        @block.tensor
        def _(eng):
            sch.emit("T", eng, sems)

        @block.vector
        def _(eng):
            sch.emit("V", eng, sems)

        @block.scalar
        def _(eng):
            sch.emit("S", eng, sems)

        @block.gpsimd
        def _(eng):
            sch.emit("G", eng, sems)
            for k in range(3):
                n_store = sch.chan_count.get(("QST", k), 0)
                if n_store:
                    eng.wait_ge(sems[("QST", k)], 16 * n_store)

        @block.sync
        def _(eng):
            sch.emit("SY", eng, sems)
    es.close()
    return nc


def kernel(L0_attn_data, L0_attn_W1, L0_attn_b1, L0_attn_W2,
           L0_ffn_U1, L0_ffn_V1, L0_ffn_b1, L0_ffn_U2, L0_ffn_V2, L0_ffn_b2):
    n = 8
    def h(a):
        a = np.asarray(a)
        assert a.dtype == np.float32, a.dtype
        return a
    xfull = np.ascontiguousarray(h(L0_attn_data))
    com = {
        "W1": np.ascontiguousarray(h(L0_attn_W1).reshape(8, 128).T),
        "b1": np.ascontiguousarray(h(L0_attn_b1).reshape(8, 128).T),
        "W2": np.ascontiguousarray(h(L0_attn_W2)),
        "U1": np.ascontiguousarray(h(L0_ffn_U1)),
        "V1": np.ascontiguousarray(h(L0_ffn_V1)),
        "fb1": np.ascontiguousarray(h(L0_ffn_b1).reshape(32, 128).T),
        "U2": np.ascontiguousarray(h(L0_ffn_U2)),
        "V2": np.ascontiguousarray(h(L0_ffn_V2)),
        "b2": np.ascontiguousarray(h(L0_ffn_b2).reshape(1, 1024)),
        "idb": np.eye(128, dtype=np.float32).astype(ml_dtypes.bfloat16),
        "idf": np.eye(128, dtype=np.float32),
    }
    nc = build_program()
    in_maps = [dict(com, x=np.ascontiguousarray(xfull[2 * c:2 * c + 2])) for c in range(n)]
    res = run_bass_kernel_spmd(nc, in_maps, core_ids=list(range(n)))
    return np.concatenate([res.results[c]["out"] for c in range(n)], axis=0)
